# version 05
# baseline (speedup 1.0000x reference)
import numpy as np
import ml_dtypes
from contextlib import ExitStack

import concourse.bass as bass
import concourse.mybir as mybir
from concourse.bass_utils import run_bass_kernel_spmd

F32 = mybir.dt.float32
BF16 = mybir.dt.bfloat16
F16 = mybir.dt.float16
U32 = mybir.dt.uint32
AF = mybir.ActivationFunctionType
ALU = mybir.AluOpType
AX = mybir.AxisListType

NCORE = 8
D = 2048
TCORE = 2048
SEQ = 4096
BATCH = 4
TC = 512
NTC = TCORE // TC
P = 128
NDC = D // P

bf = ml_dtypes.bfloat16


class _Item:
    __slots__ = ("engine", "kind", "emit", "deps", "marked", "seq", "tag", "relay_value", "dchan")

    def __init__(self, engine, kind, emit, seq, tag=""):
        self.engine = engine
        self.kind = kind
        self.emit = emit
        self.deps = {}
        self.marked = kind in ("dma", "cc")
        self.seq = seq
        self.tag = tag
        self.relay_value = None
        self.dchan = None


class ProgBuilder:
    ENGINES = ("sync", "tensor", "vector", "scalar", "gpsimd")

    def __init__(self, nc, stack, relay_engine="sync"):
        self.nc = nc
        self.stack = stack
        self.relay_engine = relay_engine
        self.items = {e: [] for e in self.ENGINES}
        self.res = {}
        self.sems = {}
        self.relays = {}
        self.n_waits = 0
        self.n_items = 0
        self.dma_count = {}
        self.auto_reads = {}

    def _edge(self, item, wch, wseq, witem):
        if witem is item:
            return
        src_engine = wch.split(".")[0]
        same = (wch == item.engine)
        if wch == "tensor" and item.engine == "tensor":
            return
        if wch == "tensor" and item.engine in ("vector", "scalar"):
            same = True
        if not same:
            if not witem.marked:
                witem.marked = True
            if wseq > item.deps.get(wch, -1):
                item.deps[wch] = wseq
            return
        if not witem.marked:
            witem.marked = True
        rlist = self.relays.setdefault(wch, [])
        echo_ch = "echo." + wch
        for rseq_src, echo_idx in rlist:
            if rseq_src >= wseq:
                if echo_idx > item.deps.get(echo_ch, -1):
                    item.deps[echo_ch] = echo_idx
                return
        reng = self.relay_engine if src_engine != self.relay_engine else "gpsimd"
        ritem = self._raw_add(reng, "relay", None, tag="relay:" + wch)
        ritem.relay_value = (wch, wseq)
        echo_idx = len(rlist)
        rlist.append((wseq, echo_idx))
        if echo_idx > item.deps.get(echo_ch, -1):
            item.deps[echo_ch] = echo_idx

    DSLOTS = 10

    def _raw_add(self, engine, kind, emit, tag=""):
        seq = len(self.items[engine])
        item = _Item(engine, kind, emit, seq, tag)
        if kind == "dma":
            n = self.dma_count.get(engine, 0)
            self.dma_count[engine] = n + 1
            item.dchan = f"{engine}.dma{n % self.DSLOTS}"
        self.items[engine].append(item)
        self.n_items += 1
        return item

    @staticmethod
    def _chan_of(item):
        return item.dchan if item.kind == "dma" else (
            item.engine + (".cc" if item.kind == "cc" else ""))

    def _touch_read(self, item, key):
        st = self.res.setdefault(key, {"w": {}, "r": {}})
        for wch, (wseq, witem) in list(st["w"].items()):
            self._edge(item, wch, wseq, witem)
        if key.startswith("pb"):
            for rch, (rseq, ritem) in list(st["r"].items()):
                if rch != item.engine and {rch, item.engine} <= {"vector", "scalar"}:
                    self._edge(item, rch, rseq, ritem)
        st["r"][self._chan_of(item)] = (item.seq, item)

    def _touch_write(self, item, key, weak=False):
        st = self.res.setdefault(key, {"w": {}, "r": {}})
        if not weak:
            for wch, (wseq, witem) in list(st["w"].items()):
                self._edge(item, wch, wseq, witem)
        for rch, (rseq, ritem) in list(st["r"].items()):
            self._edge(item, rch, rseq, ritem)
        ch = self._chan_of(item)
        if weak:
            cur = st["w"].get(ch)
            if cur is None or item.seq > cur[0]:
                st["w"][ch] = (item.seq, item)
        else:
            st["w"] = {ch: (item.seq, item)}
            st["r"] = {}

    def op(self, engine, emit, reads=(), writes=(), tag=""):
        item = self._raw_add(engine, "op", emit, tag)
        for k in reads:
            self._touch_read(item, k)
        for k in writes:
            self._touch_write(item, k)
        return item

    def dma(self, engine, out, in_, reads=(), writes=(), tag="", **kw):
        try:
            if out.dtype != in_.dtype and engine != "gpsimd":
                engine = "gpsimd"
        except AttributeError:
            pass
        nm = getattr(in_, "name", None)
        if nm in self.auto_reads:
            reads = list(reads) + [self.auto_reads[nm]]
        weak = False
        try:
            weak = "DRAM" in str(out.space).upper() or "HBM" in str(out.space).upper()
        except AttributeError:
            pass

        def emit(eng):
            return eng.dma_start(out=out, in_=in_, **kw)
        item = self._raw_add(engine, "dma", emit, tag)
        for k in reads:
            self._touch_read(item, k)
        for k in writes:
            self._touch_write(item, k, weak=weak)
        return item

    def collective(self, kind, op, replica_groups, ins, outs, reads=(), writes=(), tag="cc"):
        def emit(eng):
            return eng.collective_compute(kind, op, replica_groups=replica_groups,
                                          ins=[a.opt() for a in ins],
                                          outs=[a.opt() for a in outs])
        item = self._raw_add("gpsimd", "cc", emit, tag)
        for k in reads:
            self._touch_read(item, k)
        for k in writes:
            self._touch_write(item, k)
        return item

    def finalize(self):
        nc = self.nc
        used = set()
        for e in self.ENGINES:
            for it in self.items[e]:
                used.update(it.deps.keys())
                if it.kind == "dma":
                    used.add(it.dchan)
                if it.kind == "cc":
                    used.add(e + ".cc")
                if it.kind == "relay":
                    used.add(it.relay_value[0])
                    used.add("echo." + it.relay_value[0])
        for ch in sorted(used):
            self.sems[ch] = self.stack.enter_context(nc.semaphore(ch.replace(".", "_")))
        prefix = {}
        for ch in used:
            if ch.startswith("echo."):
                continue
            e = ch.split(".")[0]
            arr = []
            c = 0
            for it in self.items[e]:
                if ".dma" in ch:
                    if it.kind == "dma" and it.dchan == ch:
                        c += 1
                elif ch.endswith(".cc"):
                    if it.kind == "cc":
                        c += 1
                else:
                    if it.kind == "op" and it.marked:
                        c += 1
                arr.append(c)
            prefix[ch] = arr
        amount = {ch: (16 if ".dma" in ch else 1) for ch in used}
        cc_total = {e: sum(1 for it in self.items[e] if it.kind == "cc")
                    for e in self.ENGINES}

        def wait_value(ch, seq):
            if ch.startswith("echo."):
                return seq + 1
            return prefix[ch][seq] * amount[ch]

        BOOT_BIG = 1 << 28
        boot_sem = self.stack.enter_context(nc.semaphore("bootf"))
        fin_sem = self.stack.enter_context(nc.semaphore("finf"))
        used_engines = [e for e in self.ENGINES if self.items[e]]
        n_fin = len([e for e in used_engines if e != "gpsimd"])

        def emit_engine(e, eng):
            watermark = {}
            slot_uses = {}
            if e == "gpsimd":
                eng.sem_clear(boot_sem)
                eng.sem_clear(fin_sem)
                for ch in sorted(self.sems):
                    eng.sem_clear(self.sems[ch])
                eng.sem_inc(boot_sem, BOOT_BIG)
            else:
                eng.wait_ge(boot_sem, BOOT_BIG)
            for it in self.items[e]:
                for ch, seq in sorted(it.deps.items()):
                    v = wait_value(ch, seq)
                    if v > watermark.get(ch, 0):
                        eng.wait_ge(self.sems[ch], v)
                        watermark[ch] = v
                        self.n_waits += 1
                if it.kind == "relay":
                    src_ch, src_seq = it.relay_value
                    v = wait_value(src_ch, src_seq)
                    if v > watermark.get(src_ch, 0):
                        eng.wait_ge(self.sems[src_ch], v)
                        watermark[src_ch] = v
                    eng.nop().then_inc(self.sems["echo." + src_ch], 1)
                    continue
                if it.kind == "dma":
                    u = slot_uses.get(it.dchan, 0)
                    if u > 0:
                        v = 16 * u
                        if v > watermark.get(it.dchan, 0):
                            eng.wait_ge(self.sems[it.dchan], v)
                            watermark[it.dchan] = v
                    slot_uses[it.dchan] = u + 1
                inst = it.emit(eng)
                if it.kind == "dma":
                    inst.then_inc(self.sems[it.dchan], 16)
                elif it.kind == "cc":
                    inst.then_inc(self.sems[e + ".cc"], 1)
                elif it.marked and e in self.sems:
                    if e in ("vector", "scalar", "gpsimd"):
                        eng.drain().then_inc(self.sems[e], 1)
                    else:
                        inst.then_inc(self.sems[e], 1)
            for dch, u in sorted(slot_uses.items()):
                if 16 * u > watermark.get(dch, 0):
                    eng.wait_ge(self.sems[dch], 16 * u)
            if cc_total.get(e):
                eng.wait_ge(self.sems[e + ".cc"], cc_total[e])
            eng.drain()

        block = self.stack.enter_context(nc.Block())
        for e in self.ENGINES:
            if not self.items[e]:
                continue
            getattr(block, e)(lambda eng, e=e: emit_engine(e, eng))
        nc.all_engine_barrier()
        g = nc.gpsimd
        for ch in sorted(self.sems):
            g.sem_clear(self.sems[ch])
        g.sem_clear(fin_sem)
        g.sem_clear(boot_sem)
        nc.all_engine_barrier()
        return self


class Ctx:
    def __init__(self, nc, stack, pb, ins):
        self.nc = nc
        self.stack = stack
        self.pb = pb
        self.ins = ins
        sb = lambda name, shape, dt: stack.enter_context(nc.sbuf_tensor(name, shape, dt))
        self.sb = sb
        self.xt = [sb(f"xt{j}", [P, TC], F32) for j in range(NDC)]
        self.xn = [sb(f"xn{j}", [P, TC], BF16) for j in range(NDC)]
        self.hb = [sb(f"hb{j}", [P, TC], BF16) for j in range(64)]
        self.hc = [sb(f"hc{j}", [P, TC], F32) for j in range(4)]
        self.wt = [sb(f"wt{j}", [P, 1024], BF16) for j in range(8)]
        self.wf = None
        self.rows = [sb(f"row{j}", [1, TC], F32) for j in range(8)]
        self.cols = [sb(f"col{j}", [P, 16], F32) for j in range(8)]
        self.ps = [stack.enter_context(nc.psum_tensor(f"pb{j}", [P, TC], F32))
                   for j in range(8)]
        self.ones_col = sb("ones_col", [P, 1], BF16)
        self.ones_row = sb("ones_row", [1, P], BF16)
        self.ones_tc = sb("ones_tc", [1, TC], BF16)
        self.ones_col_f32 = sb("ones_col_f32", [P, 1], F32)
        self.ones_row_f32 = sb("ones_row_f32", [1, P], F32)
        pb.dma("gpsimd", out=self.ones_col[:], in_=ins["c_ones_col"], writes=["ones_col"])
        pb.dma("gpsimd", out=self.ones_row[:], in_=ins["c_ones_row"], writes=["ones_row"])
        pb.dma("gpsimd", out=self.ones_tc[:], in_=ins["c_ones_tc"], writes=["ones_tc"])
        pb.dma("gpsimd", out=self.ones_col_f32[:], in_=ins["c_ones_col_f32"], writes=["ones_col_f32"])
        pb.dma("gpsimd", out=self.ones_row_f32[:], in_=ins["c_ones_row_f32"], writes=["ones_row_f32"])
        self.eps_row = sb("eps_row", [1, 8], F32)
        self.eps_col = sb("eps_col", [P, 8], F32)
        pb.dma("gpsimd", out=self.eps_row[:], in_=ins["c_eps_row"], writes=["eps_row"])
        pb.dma("gpsimd", out=self.eps_col[:], in_=ins["c_eps_col"], writes=["eps_col"])
        self.attn_const = {}
        if "c_cmask" in ins:
            self.cmask = [sb(f"cmask{v}", [P, TC], F32) for v in range(4)]
            for v in range(4):
                pb.dma("gpsimd", out=self.cmask[v][:], in_=ins["c_cmask"][v * P:(v + 1) * P, :],
                       writes=["cmask"], tag=f"ld_cmask{v}")
        if "c_msel" in ins:
            self.msel = sb("msel", [P, 2], F32)
            pb.dma("gpsimd", out=self.msel[:], in_=ins["c_msel"], writes=["msel"])
        self.counter = 0

    def uniq(self, prefix="k"):
        self.counter += 1
        return f"{prefix}{self.counter}_"

    def stage_bf16(self, name, dt=BF16):
        src = self.ins[name]
        if src.dtype == dt:
            return src
        shp = list(src.shape)
        scr = self.nc.dram_tensor("stg_" + name, shp, dt, kind="Internal").ap()
        key = "stg_" + name
        self.pb.dma("gpsimd", out=scr, in_=src, writes=[key], tag="stage_" + name)
        self.pb.auto_reads[scr.name] = key
        self.ins[name] = scr
        return scr

    def const_sb(self, name, src_name, shape, dt):
        t = self.stack.enter_context(self.nc.sbuf_tensor(name, shape, dt))
        self.pb.dma("gpsimd", out=t[:], in_=self.ins[src_name], writes=[name])
        return t


def emit_ln_chunk(ctx, x_src_ap, t0, tag="ln"):
    pb = ctx.pb
    u = ctx.uniq(tag)
    for j in range(NDC):
        pb.dma("gpsimd", out=ctx.xt[j][:], in_=x_src_ap[j * P:(j + 1) * P, t0:t0 + TC],
               writes=[f"xt{j}"], tag=f"{u}ldx{j}")
    for j in range(NDC):
        pb.op("vector", lambda e, j=j: e.tensor_copy(out=ctx.xn[j][:], in_=ctx.xt[j][:]),
              reads=[f"xt{j}"], writes=[f"xn{j}"], tag=f"{u}cp{j}")
    for j in range(NDC):
        sqt = 48 + (j % 8)
        pb.op("scalar", lambda e, j=j, sqt=sqt: e.activation(
                  out=ctx.hb[sqt][:], in_=ctx.xt[j][:], func=AF.Square),
              reads=[f"xt{j}"], writes=[f"hb{sqt}"], tag=f"{u}sq{j}")
        pb.op("tensor", lambda e, j=j: e.matmul(ctx.ps[6][0:1, :], ctx.ones_col[:],
                                                ctx.xn[j][:], start=(j == 0), stop=(j == NDC - 1)),
              reads=[f"xn{j}", "ones_col"], writes=["pb6"], tag=f"{u}s1_{j}")
        pb.op("tensor", lambda e, j=j, sqt=sqt: e.matmul(ctx.ps[7][0:1, :], ctx.ones_col[:],
                                                         ctx.hb[sqt][:], start=(j == 0), stop=(j == NDC - 1)),
              reads=[f"hb{sqt}", "ones_col"], writes=["pb7"], tag=f"{u}s2_{j}")
    r_mean, r_ms, r_var, r_rstd, r_mr = (ctx.rows[k] for k in range(5))
    inv_d = 1.0 / D
    pb.op("vector", lambda e: e.tensor_scalar(out=r_mean[:], in0=ctx.ps[6][0:1, :],
                                              scalar1=inv_d, scalar2=None, op0=ALU.mult),
          reads=["pb6"], writes=["row0"], tag=u + "m")
    pb.op("vector", lambda e: e.tensor_scalar(out=r_ms[:], in0=ctx.ps[7][0:1, :],
                                              scalar1=inv_d, scalar2=None, op0=ALU.mult),
          reads=["pb7"], writes=["row1"], tag=u + "ms")
    pb.op("scalar", lambda e: e.activation(out=r_var[:], in_=r_mean[:], func=AF.Square),
          reads=["row0"], writes=["row2"], tag=u + "m2")
    pb.op("vector", lambda e: e.tensor_tensor(out=r_var[:], in0=r_ms[:], in1=r_var[:],
                                              op=ALU.subtract),
          reads=["row1", "row2"], writes=["row2"], tag=u + "var")
    pb.op("scalar", lambda e: e.activation(out=r_rstd[:], in_=r_var[:], func=AF.Ln,
                                           bias=ctx.eps_row[0:1, 0:1]),
          reads=["row2", "eps_row"], writes=["row3"], tag=u + "lg")
    pb.op("scalar", lambda e: e.activation(out=r_rstd[:], in_=r_rstd[:], func=AF.Exp, scale=-0.5),
          reads=["row3"], writes=["row3"], tag=u + "ex")
    pb.op("vector", lambda e: e.tensor_tensor(out=r_mr[:], in0=r_mean[:], in1=r_rstd[:],
                                              op=ALU.mult),
          reads=["row0", "row3"], writes=["row4"], tag=u + "mr")
    pb.op("tensor", lambda e: e.matmul(ctx.ps[6][:], ctx.ones_row_f32[:], r_rstd[:],
                                       start=True, stop=True),
          reads=["row3", "ones_row_f32"], writes=["pb6"], tag=u + "br")
    pb.op("tensor", lambda e: e.matmul(ctx.ps[7][:], ctx.ones_row_f32[:], r_mr[:],
                                       start=True, stop=True),
          reads=["row4", "ones_row_f32"], writes=["pb7"], tag=u + "bm")
    rstd_b, mr_b = ctx.hc[2], ctx.hc[3]
    pb.op("vector", lambda e: e.tensor_copy(out=rstd_b[:], in_=ctx.ps[6][:]),
          reads=["pb6"], writes=["hc2"], tag=u + "cb")
    pb.op("vector", lambda e: e.tensor_copy(out=mr_b[:], in_=ctx.ps[7][:]),
          reads=["pb7"], writes=["hc3"], tag=u + "cm")
    for j in range(NDC):
        tt = 56 + (j % 8)
        pb.op("vector", lambda e, j=j, tt=tt: e.tensor_tensor(
                  out=ctx.hb[tt][:], in0=ctx.xt[j][:], in1=rstd_b[:], op=ALU.mult),
              reads=[f"xt{j}", "hc2"], writes=[f"hb{tt}"], tag=f"{u}n1_{j}")
        pb.op("gpsimd", lambda e, j=j, tt=tt: e.tensor_tensor(
                  out=ctx.xn[j][:], in0=ctx.hb[tt][:], in1=mr_b[:], op=ALU.subtract),
              reads=[f"hb{tt}", "hc3"], writes=[f"xn{j}"], tag=f"{u}n2_{j}")


def emit_ternary_quant(ctx, src_ap, dst_ap, rows, cols, tag):
    pb = ctx.pb
    u = ctx.uniq(tag)
    nrc = rows // P
    ncc = cols // 1024
    acc = ctx.cols[6]
    tmp = ctx.cols[7]
    pb.op("vector", lambda e: e.memset(acc[:, 0:1], 0.0), writes=["col6"], tag=u + "z")
    nt = 0
    for rc in range(nrc):
        for cc in range(ncc):
            b = ctx.wf[nt % 2]
            key = f"wf{nt % 2}"
            pb.dma("gpsimd", out=b[:], in_=src_ap[rc * P:(rc + 1) * P, cc * 1024:(cc + 1) * 1024],
                   writes=[key], tag=f"{u}l{nt}")
            pb.op("vector", lambda e, b=b: e.tensor_reduce(
                      out=tmp[:, 0:1], in_=b[:], axis=AX.X, op=ALU.add,
                      apply_absolute_value=True),
                  reads=[key], writes=["col7"], tag=f"{u}r{nt}")
            pb.op("gpsimd", lambda e: e.tensor_tensor(out=acc[:, 0:1], in0=acc[:, 0:1],
                                                      in1=tmp[:, 0:1], op=ALU.add),
                  reads=["col6", "col7"], writes=["col6"], tag=f"{u}a{nt}")
            nt += 1
    srow = ctx.rows[6]
    pb.op("tensor", lambda e: e.matmul(ctx.ps[6][0:1, 0:1], ctx.ones_col_f32[:], acc[:, 0:1],
                                       start=True, stop=True),
          reads=["col6", "ones_col_f32"], writes=["pb6"], tag=u + "ts")
    half = 0.5 / (rows * cols)
    pb.op("vector", lambda e: e.tensor_scalar(out=srow[0:1, 0:1], in0=ctx.ps[6][0:1, 0:1],
                                              scalar1=half, scalar2=None, op0=ALU.mult),
          reads=["pb6"], writes=["row6"], tag=u + "h")
    pb.op("tensor", lambda e: e.matmul(ctx.ps[6][:, 0:1], ctx.ones_row_f32[:], srow[0:1, 0:1],
                                       start=True, stop=True),
          reads=["row6", "ones_row_f32"], writes=["pb6"], tag=u + "bc")
    s2p = ctx.cols[6]
    pb.op("vector", lambda e: e.tensor_copy(out=s2p[:, 0:1], in_=ctx.ps[6][:, 0:1]),
          reads=["pb6"], writes=["col6"], tag=u + "c0")
    pb.op("vector", lambda e: e.tensor_scalar(out=s2p[:, 1:2], in0=ctx.ps[6][:, 0:1],
                                              scalar1=-1.0, scalar2=None, op0=ALU.mult),
          reads=["pb6"], writes=["col6"], tag=u + "c1")
    nt = 0
    for rc in range(nrc):
        for cc in range(ncc):
            b = ctx.wf[nt % 2]
            key = f"wf{nt % 2}"
            t1 = ctx.wt[nt % 2]
            t1k = f"wt{nt % 2}"
            t2 = ctx.wt[2 + nt % 2]
            t2k = f"wt{2 + nt % 2}"
            pb.dma("gpsimd", out=b[:], in_=src_ap[rc * P:(rc + 1) * P, cc * 1024:(cc + 1) * 1024],
                   writes=[key], tag=f"{u}q{nt}")
            pb.op("scalar", lambda e, b=b, t1=t1: e.activation(out=t1[:], in_=b[:], func=AF.Sign,
                                                               bias=s2p[:, 1:2]),
                  reads=[key, "col6"], writes=[t1k], tag=f"{u}sA{nt}")
            pb.op("scalar", lambda e, b=b, t2=t2: e.activation(out=t2[:], in_=b[:], func=AF.Sign,
                                                               bias=s2p[:, 0:1]),
                  reads=[key, "col6"], writes=[t2k], tag=f"{u}sB{nt}")
            pb.op("vector", lambda e, t1=t1, t2=t2: e.tensor_tensor(out=t1[:], in0=t1[:], in1=t2[:],
                                                                    op=ALU.add),
                  reads=[t1k, t2k], writes=[t1k], tag=f"{u}ad{nt}")
            pb.op("gpsimd", lambda e, t1=t1: e.tensor_scalar(out=t1[:], in0=t1[:], scalar1=0.5,
                                                             scalar2=None, op0=ALU.mult),
                  reads=[t1k], writes=[t1k], tag=f"{u}mu{nt}")
            pb.dma("sync", out=dst_ap[rc * P:(rc + 1) * P, cc * 1024:(cc + 1) * 1024], in_=t1[:],
                   reads=[t1k], writes=[tag + "dst"], tag=f"{u}st{nt}")
            nt += 1


def emit_l1ffn(ctx, x_in, x_out, quantize=True):
    pb = ctx.pb
    nc = ctx.nc
    FF = 8192
    NFC = FF // P
    w1q = ctx.l1_w1q
    w2q = ctx.l1_w2q
    if quantize:
        emit_ternary_quant(ctx, ctx.ins["L1w1T"], w1q, D, FF, "q1")
        emit_ternary_quant(ctx, ctx.ins["L1w2T"], w2q, FF, D, "q2")
    b1sb = ctx.l1_b1sb
    b2sb = ctx.l1_b2sb
    for tc in range(NTC):
        t0 = tc * TC
        u = ctx.uniq(f"f1_{tc}_")
        emit_ln_chunk(ctx, x_in, t0, tag=f"l1ln{tc}_")
        if getattr(ctx, "debug_hook", None):
            ctx.debug_hook(ctx, "ln", tc)
        for g in range(8):
            for dc in range(NDC):
                wi = (g * NDC + dc) % 8
                wkey = f"wt{wi}"
                pb.dma("sync", out=ctx.wt[wi][:],
                       in_=w1q[dc * P:(dc + 1) * P, g * 1024:(g + 1) * 1024],
                       reads=["q1dst"] if quantize else (), writes=[wkey], tag=f"{u}wu{g}_{dc}")
                for ft in range(8):
                    pb.op("tensor", lambda e, wi=wi, ft=ft, dc=dc: e.matmul(
                              ctx.ps[ft][:], ctx.wt[wi][:, ft * P:(ft + 1) * P], ctx.xn[dc][:],
                              start=(dc == 0), stop=(dc == NDC - 1)),
                          reads=[wkey, f"xn{dc}"], writes=[f"pb{ft}"], tag=f"{u}mu{g}_{dc}_{ft}")
            for ft in range(8):
                fc = g * 8 + ft
                pb.op("scalar", lambda e, ft=ft, fc=fc: e.activation(
                          out=ctx.hb[fc][:], in_=ctx.ps[ft][:], func=AF.Gelu,
                          bias=b1sb[:, fc:fc + 1]),
                      reads=[f"pb{ft}", "l1b1"], writes=[f"hb{fc}"], tag=f"{u}ge{g}_{ft}")
        if getattr(ctx, "debug_hook", None):
            ctx.debug_hook(ctx, "up", tc)
        for half in range(2):
            for fc in range(NFC):
                wi = fc % 8
                wkey = f"wt{wi}"
                pb.dma("sync", out=ctx.wt[wi][:],
                       in_=w2q[fc * P:(fc + 1) * P, half * 1024:(half + 1) * 1024],
                       reads=["q2dst"] if quantize else (), writes=[wkey], tag=f"{u}wd{half}_{fc}")
                for dt in range(8):
                    pb.op("tensor", lambda e, wi=wi, dt=dt, fc=fc: e.matmul(
                              ctx.ps[dt][:], ctx.wt[wi][:, dt * P:(dt + 1) * P], ctx.hb[fc][:],
                              start=(fc == 0), stop=False),
                          reads=[wkey, f"hb{fc}"], writes=[f"pb{dt}"], tag=f"{u}md{half}_{fc}_{dt}")
            for dt in range(8):
                j = half * 8 + dt
                pb.op("tensor", lambda e, dt=dt, j=j: e.matmul(
                          ctx.ps[dt][:], b2sb[0:1, j * P:(j + 1) * P], ctx.ones_tc[:],
                          start=False, stop=True),
                      reads=["l1b2", "ones_tc"], writes=[f"pb{dt}"], tag=f"{u}bb{half}_{dt}")
            for dt in range(8):
                j = half * 8 + dt
                pb.op("vector", lambda e, dt=dt, j=j: e.scalar_tensor_tensor(
                          out=ctx.xt[j][:], in0=ctx.ps[dt][:], scalar=0.5, in1=ctx.xt[j][:],
                          op0=ALU.mult, op1=ALU.add),
                      reads=[f"pb{dt}", f"xt{j}"], writes=[f"xt{j}"], tag=f"{u}rs{half}_{dt}")
                pb.dma("gpsimd", out=x_out[j * P:(j + 1) * P, t0:t0 + TC], in_=ctx.xt[j][:],
                       reads=[f"xt{j}"], writes=["xout"], tag=f"{u}so{half}_{dt}")


def setup_l1ffn_consts(ctx):
    if ctx.wf is None:
        ctx.wf = [ctx.sb(f"wf{j}", [P, 1024], F32) for j in range(2)]
    ctx.l1_w1q = ctx.nc.dram_tensor("l1w1q", [D, 8192], BF16, kind="Internal").ap()
    ctx.l1_w2q = ctx.nc.dram_tensor("l1w2q", [8192, D], BF16, kind="Internal").ap()
    ctx.l1_b1sb = ctx.sb("l1b1", [P, 64], F32)
    ctx.pb.dma("gpsimd", out=ctx.l1_b1sb[:], in_=ctx.ins["L1b1T"], writes=["l1b1"])
    ctx.l1_b2sb = ctx.sb("l1b2", [1, D], BF16)
    ctx.pb.dma("gpsimd", out=ctx.l1_b2sb[:], in_=ctx.ins["L1b2row"], writes=["l1b2"])


PAIRS = [[0, 1], [2, 3], [4, 5], [6, 7]]
NEG30 = -1.0e30



def emit_attention(ctx, x_in, x_out, cfg):
    pb = ctx.pb
    nc = ctx.nc
    L = cfg["name"]
    HD = cfg["hd"]
    NHD = HD // P
    NHL = cfg["nh_local"]
    KVR = cfg.get("kv_ratio", 1)
    NKVL = NHL // KVR
    CQ = NHL * HD
    CKV = NKVL * HD
    causal = cfg.get("causal", True)
    scale = cfg["scale"]
    softcap = cfg.get("softcap", None)
    rope_d2 = cfg.get("rope_d2", None)
    relu_qkv = cfg.get("relu_qkv", False)
    u0 = ctx.uniq(L + "_")

    dram = lambda n, shp, dt: nc.dram_tensor(u0 + n, shp, dt, kind="Internal").ap()
    xnb = dram("xnb", [D, TCORE], BF16)
    xng = dram("xng", [2 * D, TCORE], BF16)
    qT = dram("qT", [CQ, SEQ], BF16)
    kT = dram("kT", [CKV, SEQ], BF16)
    vB = dram("vB", [SEQ, CKV], BF16)
    oS0 = dram("oS0", [CQ, TCORE], BF16)
    oS1 = dram("oS1", [CQ, TCORE], BF16)
    OG0 = dram("OG0", [2 * CQ, TCORE], BF16)
    OG1 = dram("OG1", [2 * CQ, TCORE], BF16)
    KXB, KXG, KQ, KK, KV, KOS, KOR = (u0 + n for n in
                                      ("xnb", "xng", "qT", "kT", "vB", "oS", "oR"))

    for tc in range(NTC):
        t0 = tc * TC
        emit_ln_chunk(ctx, x_in, t0, tag=f"{u0}ln{tc}_")
        for j in range(NDC):
            pb.dma("gpsimd", out=xnb[j * P:(j + 1) * P, t0:t0 + TC], in_=ctx.xn[j][:],
                   reads=[f"xn{j}"], writes=[KXB], tag=f"{u0}sx{tc}_{j}")

    pb.collective("AllGather", ALU.bypass, PAIRS, [xnb], [xng],
                  reads=[KXB], writes=[KXG], tag=u0 + "agx")

    mk_row = ctx.rows[6]
    pb.op("vector", lambda e: e.memset(mk_row[:], 0.0), writes=["row6"], tag=u0 + "z6")
    if rope_d2 is not None:
        cosT = ctx.attn_const[L]["cos"]
        sinT = ctx.attn_const[L]["sin"]

    def proj_evac(ug, which, ct, bank, g, dst, kdst, bias_sb):
        s0 = g * TC
        ev = ctx.hb[58 + (ct % 2)]
        evk = f"hb{58 + (ct % 2)}"
        if relu_qkv:
            pb.op("scalar", lambda e: e.activation(out=ctx.hc[ct % 2][:], in_=ctx.ps[bank][:],
                                                   func=AF.Relu, bias=bias_sb[:, ct:ct + 1]),
                  reads=[f"pb{bank}", "attnc"], writes=[f"hc{ct % 2}"], tag=f"{ug}pe{which}{ct}")
        else:
            pb.op("scalar", lambda e: e.copy(out=ctx.hc[ct % 2][:], in_=ctx.ps[bank][:]),
                  reads=[f"pb{bank}"], writes=[f"hc{ct % 2}"], tag=f"{ug}pe{which}{ct}")
        src = ctx.hc[ct % 2]
        srck = f"hc{ct % 2}"
        if rope_d2 == 64:
            ev0 = ctx.hb[58 + (ct % 2)]
            ev0k = f"hb{58 + (ct % 2)}"
            pb.op("scalar", lambda e, ev0=ev0: e.copy(out=ev0[:], in_=src[:]),
                  reads=[srck], writes=[ev0k], tag=f"{ug}rc{which}{ct}")
            pb.op("tensor", lambda e, ev0=ev0: e.matmul(ctx.ps[4][:], ctx.swap64[:], ev0[:],
                                                        start=True, stop=True),
                  reads=[ev0k, "swap64"], writes=["pb4"], tag=f"{ug}rw{which}{ct}")
            t1 = ctx.hc[2]
            pb.op("vector", lambda e, ev0=ev0: e.tensor_tensor(
                      out=t1[:], in0=ev0[:], in1=cosT[:, s0:s0 + TC], op=ALU.mult),
                  reads=[ev0k, "attnc"], writes=["hc2"], tag=f"{ug}r1{which}{ct}")
            pb.op("vector", lambda e: e.tensor_tensor(
                      out=ctx.hc[3][:], in0=ctx.ps[4][:], in1=sinT[:, s0:s0 + TC], op=ALU.mult),
                  reads=["pb4", "attnc"], writes=["hc3"], tag=f"{ug}r2{which}{ct}")
            pb.op("vector", lambda e: e.tensor_tensor(out=t1[:], in0=t1[:], in1=ctx.hc[3][:],
                                                      op=ALU.add),
                  reads=["hc2", "hc3"], writes=["hc2"], tag=f"{ug}r3{which}{ct}")
            src, srck = t1, "hc2"
        pb.op("vector", lambda e: e.tensor_copy(out=ev[:], in_=src[:]),
              reads=[srck], writes=[evk], tag=f"{ug}pc{which}{ct}")
        pb.dma("gpsimd", out=dst[ct * P:(ct + 1) * P, s0:s0 + TC], in_=ev[:],
               reads=[evk], writes=[kdst], tag=f"{ug}ps{which}{ct}")
        return ev, evk

    def proj_evac_pair(ug, which, ct0, g, dst, kdst):
        s0 = g * TC
        cs = cosT[:, s0:s0 + TC]
        sn = sinT[:, s0:s0 + TC]
        a, b = ctx.hc[0], ctx.hc[1]
        o1, o2 = ctx.hc[2], ctx.hc[3]
        pb.op("scalar", lambda e: e.copy(out=a[:], in_=ctx.ps[0][:]),
              reads=["pb0"], writes=["hc0"], tag=f"{ug}pa{which}{ct0}")
        pb.op("scalar", lambda e: e.copy(out=b[:], in_=ctx.ps[1][:]),
              reads=["pb1"], writes=["hc1"], tag=f"{ug}pb{which}{ct0}")
        pb.op("vector", lambda e: e.tensor_tensor(out=o1[:], in0=a[:], in1=cs, op=ALU.mult),
              reads=["hc0", "attnc"], writes=["hc2"], tag=f"{ug}q1{which}{ct0}")
        pb.op("gpsimd", lambda e: e.tensor_tensor(out=o2[:], in0=b[:], in1=cs, op=ALU.mult),
              reads=["hc1", "attnc"], writes=["hc3"], tag=f"{ug}q2{which}{ct0}")
        t = ctx.hb[58]
        pb.op("vector", lambda e: e.tensor_tensor(out=t[:], in0=b[:], in1=sn, op=ALU.mult),
              reads=["hc1", "attnc"], writes=["hb58"], tag=f"{ug}q4{which}{ct0}")
        pb.op("vector", lambda e: e.tensor_tensor(out=o1[:], in0=o1[:], in1=t[:], op=ALU.subtract),
              reads=["hc2", "hb58"], writes=["hc2"], tag=f"{ug}q5{which}{ct0}")
        pb.op("gpsimd", lambda e: e.tensor_tensor(out=t[:], in0=a[:], in1=sn, op=ALU.mult),
              reads=["hc0", "attnc"], writes=["hb58"], tag=f"{ug}q6{which}{ct0}")
        pb.op("gpsimd", lambda e: e.tensor_tensor(out=o2[:], in0=o2[:], in1=t[:], op=ALU.add),
              reads=["hc3", "hb58"], writes=["hc3"], tag=f"{ug}q7{which}{ct0}")
        ev1, ev2 = ctx.hb[59], ctx.hb[60]
        pb.op("vector", lambda e: e.tensor_copy(out=ev1[:], in_=o1[:]),
              reads=["hc2"], writes=["hb59"], tag=f"{ug}q8{which}{ct0}")
        pb.op("vector", lambda e: e.tensor_copy(out=ev2[:], in_=o2[:]),
              reads=["hc3"], writes=["hb60"], tag=f"{ug}q9{which}{ct0}")
        pb.dma("gpsimd", out=dst[ct0 * P:(ct0 + 1) * P, s0:s0 + TC], in_=ev1[:],
               reads=["hb59"], writes=[kdst], tag=f"{ug}qs1{which}{ct0}")
        pb.dma("gpsimd", out=dst[(ct0 + 1) * P:(ct0 + 2) * P, s0:s0 + TC], in_=ev2[:],
               reads=["hb60"], writes=[kdst], tag=f"{ug}qs2{which}{ct0}")
        return [(ev1, "hb59"), (ev2, "hb60")]

    nq_t = CQ // P
    nkv_t = CKV // P
    for g in range(SEQ // TC):
        ug = f"{u0}c{g}_"
        rnk, lc = g // NTC, g % NTC
        for j in range(NDC):
            pb.dma("sync", out=ctx.xn[j][:],
                   in_=xng[rnk * D + j * P:rnk * D + (j + 1) * P, lc * TC:lc * TC + TC],
                   reads=[KXG], writes=[f"xn{j}"], tag=f"{ug}lx{j}")

        for which, wname, nt, dst, kdst, bias_key in (
                ("k", cfg["wk"], nkv_t, kT, KK, "bk"),
                ("q", cfg["wq"], nq_t, qT, KQ, "bq")):
            wap = ctx.ins[wname]
            bias_sb = ctx.attn_const[L].get(bias_key) if relu_qkv else None
            ct = 0
            while ct < nt:
                pair = (rope_d2 == 128)
                nct = 2 if pair else 1
                for sub in range(nct):
                    bank = sub if pair else (ct + sub) % 2
                    for dc in range(NDC):
                        wi = dc % 4
                        pb.dma("sync", out=ctx.wt[wi][:, 0:P],
                               in_=wap[dc * P:(dc + 1) * P, (ct + sub) * P:(ct + sub + 1) * P],
                               writes=[f"wt{wi}"], tag=f"{ug}w{which}{ct + sub}_{dc}")
                        pb.op("tensor", lambda e, wi=wi, dc=dc, bank=bank: e.matmul(
                                  ctx.ps[bank][:], ctx.wt[wi][:, 0:P], ctx.xn[dc][:],
                                  start=(dc == 0), stop=(dc == NDC - 1)),
                              reads=[f"wt{wi}", f"xn{dc}"], writes=[f"pb{bank}"],
                              tag=f"{ug}m{which}{ct + sub}_{dc}")
                if pair:
                    evs = proj_evac_pair(ug, which, ct, g, dst, kdst)
                else:
                    evs = [proj_evac(ug, which, ct, ct % 2, g, dst, kdst, bias_sb)]
                if which == "k":
                    for si, (ev, evk) in enumerate(evs):
                        cti = ct + si
                        kvh = cti // NHD
                        sq = ctx.hb[61]
                        pb.op("scalar", lambda e, ev=ev: e.activation(out=sq[:], in_=ev[:],
                                                                      func=AF.Square),
                              reads=[evk], writes=["hb61"], tag=f"{ug}k2{cti}")
                        pb.op("tensor", lambda e: e.matmul(ctx.ps[5][0:1, :], ctx.ones_col[:],
                                                           sq[:], start=(cti % NHD == 0),
                                                           stop=(cti % NHD == NHD - 1)),
                              reads=["hb61", "ones_col"], writes=["pb5"], tag=f"{ug}k2m{cti}")
                        if cti % NHD == NHD - 1:
                            r7 = ctx.rows[7]
                            pb.op("vector", lambda e: e.reduce_max(out=r7[0:1, 0:1],
                                                                   in_=ctx.ps[5][0:1, :], axis=AX.X),
                                  reads=["pb5"], writes=["row7"], tag=f"{ug}k2r{cti}")
                            pb.op("vector", lambda e, kvh=kvh: e.tensor_tensor(
                                      out=mk_row[0:1, 32 + kvh:33 + kvh],
                                      in0=mk_row[0:1, 32 + kvh:33 + kvh],
                                      in1=r7[0:1, 0:1], op=ALU.max),
                                  reads=["row6", "row7"], writes=["row6"], tag=f"{ug}k2x{cti}")
                ct += nct

        nvc = (CKV + TC - 1) // TC
        for tt in range(TC // P):
            for half in range(nvc):
                w0c = half * TC
                wid = min(TC, CKV - w0c)
                bank = 2 + (tt % 2)
                laststop = not relu_qkv
                for dc in range(NDC):
                    wi = 4 + dc % 4
                    pb.dma("sync", out=ctx.wt[wi][:, 0:wid],
                           in_=ctx.ins[cfg["wv"]][dc * P:(dc + 1) * P, w0c:w0c + wid],
                           writes=[f"wt{wi}"], tag=f"{ug}wv{tt}_{half}_{dc}")
                    pb.op("tensor", lambda e, wi=wi, dc=dc, tt=tt, bank=bank, wid=wid,
                          st=(laststop and dc == NDC - 1): e.matmul(
                              ctx.ps[bank][:, 0:wid], ctx.xn[dc][:, tt * P:(tt + 1) * P],
                              ctx.wt[wi][:, 0:wid], start=(dc == 0), stop=st),
                          reads=[f"wt{wi}", f"xn{dc}"], writes=[f"pb{bank}"],
                          tag=f"{ug}mv{tt}_{half}_{dc}")
                ev = ctx.hb[62 + (tt % 2)]
                if relu_qkv:
                    bvrow = ctx.attn_const[L]["bv"]
                    pb.op("tensor", lambda e, bank=bank, w0c=w0c, wid=wid: e.matmul(
                              ctx.ps[bank][:, 0:wid], ctx.ones_row[0:1, :],
                              bvrow[0:1, w0c:w0c + wid],
                              start=False, stop=True),
                          reads=["attnc", "ones_row"], writes=[f"pb{bank}"],
                          tag=f"{ug}bv{tt}_{half}")
                    pb.op("scalar", lambda e, ev=ev, bank=bank, wid=wid: e.activation(
                              out=ev[:, 0:wid], in_=ctx.ps[bank][:, 0:wid], func=AF.Relu),
                          reads=[f"pb{bank}"], writes=[f"hb{62 + (tt % 2)}"],
                          tag=f"{ug}ev{tt}_{half}")
                else:
                    pb.op("scalar", lambda e, ev=ev, bank=bank, wid=wid: e.copy(
                              out=ev[:, 0:wid], in_=ctx.ps[bank][:, 0:wid]),
                          reads=[f"pb{bank}"], writes=[f"hb{62 + (tt % 2)}"],
                          tag=f"{ug}ev{tt}_{half}")
                pb.dma("gpsimd", out=vB[g * TC + tt * P:g * TC + (tt + 1) * P, w0c:w0c + wid],
                       in_=ev[:, 0:wid], reads=[f"hb{62 + (tt % 2)}"], writes=[KV],
                       tag=f"{ug}sv{tt}_{half}")

    return dict(qT=qT, kT=kT, vB=vB, oS=(oS0, oS1), OG=(OG0, OG1), xng=xng,
                KQ=KQ, KK=KK, KV=KV, KOS=KOS, KOR=KOR, u0=u0, cfg=cfg,
                mk_row=mk_row)


def emit_attention_core(ctx, A):
    pb = ctx.pb
    cfg = A["cfg"]
    L = cfg["name"]
    HD = cfg["hd"]
    NHD = HD // P
    NHL = cfg["nh_local"]
    KVR = cfg.get("kv_ratio", 1)
    CQ = NHL * HD
    causal = cfg.get("causal", True)
    scale = cfg["scale"]
    softcap = cfg.get("softcap", None)
    u0 = A["u0"]
    qT, kT, vB = A["qT"], A["kT"], A["vB"]
    KQ, KK, KV, KOS = A["KQ"], A["KK"], A["KV"], A["KOS"]
    cache_kv = NHD <= 2
    if L == "l0a":
        VC, KG, QB = 32, 48, 52
    elif L == "l3a":
        VC, KG, QB = None, 40, 32
    else:
        VC, KG, QB = 32, 40, 44
    NKT = SEQ // P
    cached_kv = [None]
    gating = cfg.get("gating", False)

    for qh in range(NHL):
        kvh = qh // KVR
        uh = f"{u0}h{qh}_"
        if gating:
            for i8 in range(8):
                pb.dma("sync", out=ctx.hb[40 + i8][:],
                       in_=kT[kvh * P:(kvh + 1) * P, i8 * TC:(i8 + 1) * TC],
                       reads=[A["KK"]], writes=[f"hb{40 + i8}"], tag=f"{uh}km{i8}")
                pb.op("vector", lambda e, i8=i8: e.tensor_reduce(
                          out=ctx.l2km[:, 2 * i8:2 * i8 + 2],
                          in_=ctx.hb[40 + i8][:].rearrange("p (c w) -> p c w", c=2),
                          axis=AX.X, op=ALU.add),
                      reads=[f"hb{40 + i8}"], writes=["l2km"], tag=f"{uh}kr{i8}")
            pb.op("vector", lambda e: e.tensor_scalar(out=ctx.l2km[:], in0=ctx.l2km[:],
                                                      scalar1=1.0 / 256.0, scalar2=None,
                                                      op0=ALU.mult),
                  reads=["l2km"], writes=["l2km"], tag=uh + "ks")
            pb.op("scalar", lambda e: e.copy(out=ctx.l2kmb[:], in_=ctx.l2km[:]),
                  reads=["l2km"], writes=["l2kmb"], tag=uh + "kc")
            for qt in range(SEQ // P):
                uqt = f"{uh}g{qt}_"
                pb.dma("sync", out=ctx.hb[48][:, 0:P],
                       in_=qT[qh * P:(qh + 1) * P, qt * P:(qt + 1) * P],
                       reads=[A["KQ"]], writes=["hb48"], tag=uqt + "lq")
                pb.op("tensor", lambda e, qt=qt: e.matmul(ctx.ps[5][:, 0:16], ctx.hb[48][:, 0:P],
                                                          ctx.l2kmb[:], start=True, stop=True),
                      reads=["hb48", "l2kmb"], writes=["pb5"], tag=uqt + "gm")
                lt = ctx.cols[4]
                pb.op("vector", lambda e, qt=qt: e.tensor_tensor(
                          out=lt[:, 0:16], in0=ctx.ps[5][:, 0:16],
                          in1=ctx.l2gm[:, qt * 16:(qt + 1) * 16], op=ALU.add),
                      reads=["pb5", "l2gm"], writes=["col4"], tag=uqt + "ga")
                tv, ti, mxc = ctx.cols[5], ctx.l3_idx, ctx.cols[6]
                pb.op("vector", lambda e: e.max(out=tv[:, 0:8], in_=lt[:, 0:16]),
                      reads=["col4"], writes=["col5"], tag=uqt + "t8")
                pb.op("vector", lambda e: e.max_index(out=ti[:], in_max=tv[:, 0:8],
                                                      in_values=lt[:, 0:16]),
                      reads=["col5", "col4"], writes=["l3idx"], tag=uqt + "ti")
                sel = ctx.cols[7]
                pb.op("vector", lambda e: e.memset(sel[:, 0:16], 0.0),
                      writes=["col7"], tag=uqt + "sz")
                for jj in range(4):
                    pb.op("vector", lambda e, jj=jj: e.tensor_copy(out=mxc[:, jj:jj + 1],
                                                                   in_=ti[:, jj:jj + 1]),
                          reads=["l3idx"], writes=["col6"], tag=uqt + f"ic{jj}")
                    pb.op("vector", lambda e, jj=jj: e.tensor_scalar(
                              out=mxc[:, 8 + jj:9 + jj], in0=tv[:, jj:jj + 1], scalar1=-5e29,
                              scalar2=None, op0=ALU.is_gt),
                          reads=["col5", "col6"], writes=["col6"], tag=uqt + f"iv{jj}")
                for jj in range(4):
                    oh = ctx.cols[4]
                    pb.op("vector", lambda e, jj=jj: e.tensor_scalar(
                              out=ctx.hc[0][:, 0:16], in0=ctx.iota16[:, 0:16],
                              scalar1=mxc[:, jj:jj + 1], scalar2=mxc[:, 8 + jj:9 + jj],
                              op0=ALU.is_equal, op1=ALU.mult),
                          reads=["iota16", "col6"], writes=["hc0"], tag=uqt + f"oh{jj}")
                    pb.op("vector", lambda e: e.tensor_tensor(out=sel[:, 0:16], in0=sel[:, 0:16],
                                                              in1=ctx.hc[0][:, 0:16], op=ALU.max),
                          reads=["col7", "hc0"], writes=["col7"], tag=uqt + f"os{jj}")
                pb.op("vector", lambda e: e.tensor_scalar(out=ctx.hc[0][:, 0:16], in0=sel[:, 0:16],
                                                          scalar1=-1.0, scalar2=1.0,
                                                          op0=ALU.mult, op1=ALU.add),
                      reads=["col7"], writes=["hc0"], tag=uqt + "ns")
                pb.op("tensor", lambda e, qt=qt: e.transpose(ctx.ps[7][0:16, 0:P],
                                                             ctx.hc[0][:, 0:16],
                                                             ctx.ident_f32[:]),
                      reads=["hc0", "identf"], writes=["pb7"], tag=uqt + "tr")
                pb.op("vector", lambda e, qt=qt: e.tensor_copy(
                          out=ctx.l2selT[0:16, qt * P:(qt + 1) * P], in_=ctx.ps[7][0:16, 0:P]),
                      reads=["pb7"], writes=["l2selT"], tag=uqt + "sc")
        if cache_kv and cached_kv[0] != kvh:
            cached_kv[0] = kvh
            per_tile = 512 // HD
            for kt in range(NKT):
                hbi = VC + kt // per_tile
                off = (kt % per_tile) * HD
                pb.dma("sync", out=ctx.hb[hbi][:, off:off + HD],
                       in_=vB[kt * P:(kt + 1) * P, kvh * HD:(kvh + 1) * HD],
                       reads=[KV], writes=[f"hb{hbi}"], tag=f"{uh}vc{kt}")
        for qc in range(SEQ // TC):
            uq = f"{uh}q{qc}_"
            s0 = qc * TC
            kt_max = (qc + 1) * 4 if causal else NKT
            if gating:
                pb.dma("gpsimd", out=ctx.l2scr16, in_=ctx.l2selT[0:16, s0:s0 + TC],
                       reads=["l2selT"], writes=["l2scr"], tag=uq + "mr1")
                pb.dma("gpsimd", out=ctx.l2mr[0:1, :],
                       in_=ctx.l2scr16.rearrange("c q -> (c q)").unsqueeze(0),
                       reads=["l2scr"], writes=["l2mr"], tag=uq + "mr2")
            for i in range(NHD):
                pb.dma("sync", out=ctx.hb[QB + i][:],
                       in_=qT[(qh * NHD + i) * P:(qh * NHD + i + 1) * P, s0:s0 + TC],
                       reads=[KQ], writes=[f"hb{QB + i}"], tag=f"{uq}lq{i}")
            for i in range(NHD):
                pb.op("scalar", lambda e, i=i: e.activation(out=ctx.hb[61][:],
                                                            in_=ctx.hb[QB + i][:], func=AF.Square),
                      reads=[f"hb{QB + i}"], writes=["hb61"], tag=f"{uq}sq{i}")
                pb.op("tensor", lambda e, i=i: e.matmul(ctx.ps[5][0:1, :], ctx.ones_col[:],
                                                        ctx.hb[61][:], start=(i == 0),
                                                        stop=(i == NHD - 1)),
                      reads=["hb61", "ones_col"], writes=["pb5"], tag=f"{uq}sm{i}")
            r7 = ctx.rows[7]
            pb.op("vector", lambda e: e.reduce_max(out=r7[0:1, 0:1], in_=ctx.ps[5][0:1, :],
                                                   axis=AX.X),
                  reads=["pb5"], writes=["row7"], tag=uq + "qm")
            pb.op("vector", lambda e, kvh=kvh: e.tensor_tensor(
                      out=r7[0:1, 1:2], in0=r7[0:1, 0:1],
                      in1=A["mk_row"][0:1, 32 + kvh:33 + kvh], op=ALU.mult),
                  reads=["row7", "row6"], writes=["row7"], tag=uq + "c2")
            pb.op("scalar", lambda e: e.activation(out=r7[0:1, 2:3], in_=r7[0:1, 1:2],
                                                   func=AF.Sqrt),
                  reads=["row7"], writes=["row7"], tag=uq + "sB")
            if softcap is not None:
                pb.op("scalar", lambda e: e.activation(out=r7[0:1, 3:4], in_=r7[0:1, 2:3],
                                                       func=AF.Tanh, scale=scale / softcap),
                      reads=["row7"], writes=["row7"], tag=uq + "th")
                pb.op("vector", lambda e: e.tensor_scalar(out=r7[0:1, 4:5], in0=r7[0:1, 3:4],
                                                          scalar1=-softcap, scalar2=None,
                                                          op0=ALU.mult),
                      reads=["row7"], writes=["row7"], tag=uq + "nc")
            else:
                pb.op("vector", lambda e: e.tensor_scalar(out=r7[0:1, 4:5], in0=r7[0:1, 2:3],
                                                          scalar1=-scale, scalar2=None,
                                                          op0=ALU.mult),
                      reads=["row7"], writes=["row7"], tag=uq + "nc")
            pb.op("tensor", lambda e: e.matmul(ctx.ps[7][:, 0:1], ctx.ones_row_f32[:],
                                               r7[0:1, 4:5], start=True, stop=True),
                  reads=["row7", "ones_row_f32"], writes=["pb7"], tag=uq + "bc")
            negc = ctx.cols[3]
            pb.op("vector", lambda e: e.tensor_copy(out=negc[:, 0:1], in_=ctx.ps[7][:, 0:1]),
                  reads=["pb7"], writes=["col3"], tag=uq + "cc")
            for g0 in range(0, kt_max, 8):
                gkt = min(8, kt_max - g0)
                ncols = gkt * P
                ktiles = {}
                for i in range(NHD):
                    for cc in range((ncols + TC - 1) // TC):
                        w = min(TC, ncols - cc * TC)
                        hbi = KG + i * 2 + cc if L != "l3a" else KG + i * 2 + cc
                        pb.dma("sync", out=ctx.hb[hbi][:, 0:w],
                               in_=kT[(kvh * NHD + i) * P:(kvh * NHD + i + 1) * P,
                                      g0 * P + cc * TC:g0 * P + cc * TC + w],
                               reads=[KK], writes=[f"hb{hbi}"], tag=f"{uq}kg{g0}_{i}_{cc}")
                for kt in range(g0, g0 + gkt):
                    bank = kt % 2
                    off = (kt - g0) * P
                    for i in range(NHD):
                        hbi = KG + i * 2 + off // TC
                        pb.op("tensor", lambda e, i=i, hbi=hbi, o=off % TC, bank=bank: e.matmul(
                                  ctx.ps[bank][:], ctx.hb[hbi][:, o:o + P],
                                  ctx.hb[QB + i][:], start=(i == 0),
                                  stop=(i == NHD - 1 and not gating)),
                              reads=[f"hb{hbi}", f"hb{QB + i}"], writes=[f"pb{bank}"],
                              tag=f"{uq}sc{kt}_{i}")
                    if gating:
                        ct = kt // 2
                        pb.op("tensor", lambda e, bank=bank, ct=ct: e.matmul(
                                  ctx.ps[bank][:], ctx.neg_row[:],
                                  ctx.l2mr[0:1, ct * TC:(ct + 1) * TC],
                                  start=False, stop=True),
                              reads=["l2mr", "neg_row"], writes=[f"pb{bank}"],
                              tag=f"{uq}sm{kt}")
                    if causal and kt >= qc * 4:
                        v = kt - qc * 4
                        pb.op("vector", lambda e, bank=bank, v=v: e.tensor_tensor(
                                  out=ctx.ps[bank][:], in0=ctx.ps[bank][:],
                                  in1=ctx.cmask[v][:], op=ALU.add),
                              reads=[f"pb{bank}", "cmask"], writes=[f"pb{bank}"],
                              tag=f"{uq}mk{kt}")
                    if softcap is not None:
                        pb.op("scalar", lambda e, bank=bank: e.activation(
                                  out=ctx.hc[2][:], in_=ctx.ps[bank][:], func=AF.Tanh,
                                  scale=scale / softcap),
                              reads=[f"pb{bank}"], writes=["hc2"], tag=f"{uq}tc{kt}")
                        pb.op("scalar", lambda e, kt=kt: e.activation(
                                  out=ctx.hb[kt][:], in_=ctx.hc[2][:], func=AF.Exp,
                                  scale=softcap, bias=negc[:, 0:1]),
                              reads=["hc2", "col3"], writes=[f"hb{kt}"], tag=f"{uq}ex{kt}")
                    else:
                        pb.op("scalar", lambda e, kt=kt, bank=bank: e.activation(
                                  out=ctx.hb[kt][:], in_=ctx.ps[bank][:], func=AF.Exp,
                                  scale=scale, bias=negc[:, 0:1]),
                              reads=[f"pb{bank}", "col3"], writes=[f"hb{kt}"],
                              tag=f"{uq}ex{kt}")
            for kt in range(kt_max):
                pb.op("tensor", lambda e, kt=kt: e.matmul(ctx.ps[6][0:1, :], ctx.ones_col[:],
                                                          ctx.hb[kt][:], start=(kt == 0),
                                                          stop=(kt == kt_max - 1)),
                      reads=[f"hb{kt}", "ones_col"], writes=["pb6"], tag=f"{uq}dn{kt}")
            npass = (NHD + 3) // 4
            for pp in range(npass):
                h0 = pp * 4
                nh = min(4, NHD - h0)
                for kt in range(kt_max):
                    if cache_kv:
                        per_tile = 512 // HD
                        hbi = VC + kt // per_tile
                        voff = (kt % per_tile) * HD
                        vk = f"hb{hbi}"
                        vt = ctx.hb[hbi]
                    else:
                        wi = 4 + kt % 4
                        vt = ctx.wt[wi]
                        vk = f"wt{wi}"
                        voff = 0
                        pb.dma("sync", out=vt[:, 0:nh * P],
                               in_=vB[kt * P:(kt + 1) * P,
                                      kvh * HD + h0 * P:kvh * HD + (h0 + nh) * P],
                               writes=[vk], tag=f"{uq}vl{pp}_{kt}")
                    for i in range(nh):
                        bank = 2 + i
                        pb.op("tensor", lambda e, vt=vt, voff=voff, i=i, h0=h0, kt=kt, bank=bank: e.matmul(
                                  ctx.ps[bank][:], vt[:, voff + i * P:voff + i * P + P],
                                  ctx.hb[kt][:], start=(kt == 0), stop=(kt == kt_max - 1)),
                              reads=[vk, f"hb{kt}"], writes=[f"pb{bank}"],
                              tag=f"{uq}pv{pp}_{kt}_{i}")
                if pp == 0:
                    r5 = ctx.rows[5]
                    pb.op("vector", lambda e: e.reciprocal(out=r5[0:1, :], in_=ctx.ps[6][0:1, :]),
                          reads=["pb6"], writes=["row5"], tag=uq + "rd")
                    pb.op("tensor", lambda e: e.matmul(ctx.ps[7][:], ctx.ones_row_f32[:],
                                                       r5[0:1, :], start=True, stop=True),
                          reads=["row5", "ones_row_f32"], writes=["pb7"], tag=uq + "rb")
                    pb.op("vector", lambda e: e.tensor_copy(out=ctx.hc[2][:], in_=ctx.ps[7][:]),
                          reads=["pb7"], writes=["hc2"], tag=uq + "rc")
                if cfg.get("o_rms") and pp == 0:
                    pb.op("scalar", lambda e: e.activation(out=ctx.hb[61][:], in_=ctx.ps[2][:],
                                                           func=AF.Square),
                          reads=["pb2"], writes=["hb61"], tag=uq + "mq")
                    pb.op("tensor", lambda e: e.matmul(ctx.ps[5][0:1, :], ctx.ones_col[:],
                                                       ctx.hb[61][:], start=True, stop=True),
                          reads=["hb61", "ones_col"], writes=["pb5"], tag=uq + "mqm")
                    r4, r5 = ctx.rows[4], ctx.rows[5]
                    pb.op("vector", lambda e: e.tensor_scalar(out=r4[:], in0=ctx.ps[5][0:1, :],
                                                              scalar1=1.0 / HD, scalar2=None,
                                                              op0=ALU.mult),
                          reads=["pb5"], writes=["row4"], tag=uq + "mq2")
                    pb.op("vector", lambda e: e.tensor_tensor(out=r4[:], in0=r4[:], in1=r5[:],
                                                              op=ALU.mult),
                          reads=["row4", "row5"], writes=["row4"], tag=uq + "mq3")
                    pb.op("vector", lambda e: e.tensor_tensor(out=r4[:], in0=r4[:], in1=r5[:],
                                                              op=ALU.mult),
                          reads=["row4", "row5"], writes=["row4"], tag=uq + "mq4")
                    pb.op("scalar", lambda e: e.activation(out=r4[:], in_=r4[:], func=AF.Ln,
                                                           bias=ctx.eps_row[0:1, 1:2]),
                          reads=["row4", "eps_row"], writes=["row4"], tag=uq + "mq5")
                    pb.op("scalar", lambda e: e.activation(out=r4[:], in_=r4[:], func=AF.Exp,
                                                           scale=-0.5),
                          reads=["row4"], writes=["row4"], tag=uq + "mq6")
                    pb.op("vector", lambda e: e.tensor_tensor(out=r4[:], in0=r4[:], in1=r5[:],
                                                              op=ALU.mult),
                          reads=["row4", "row5"], writes=["row4"], tag=uq + "mq7")
                    pb.op("tensor", lambda e: e.matmul(ctx.ps[7][:], ctx.ones_row_f32[:], r4[:],
                                                       start=True, stop=True),
                          reads=["row4", "ones_row_f32"], writes=["pb7"], tag=uq + "mq8")
                    pb.op("vector", lambda e: e.tensor_copy(out=ctx.hc[2][:], in_=ctx.ps[7][:]),
                          reads=["pb7"], writes=["hc2"], tag=uq + "mq9")
                for i in range(nh):
                    bank = 2 + i
                    hd_i = h0 + i
                    ev = ctx.hb[56 + (hd_i % 4)]
                    evk = f"hb{56 + (hd_i % 4)}"
                    if cfg.get("o_rms"):
                        pb.op("vector", lambda e, bank=bank: e.tensor_tensor(
                                  out=ctx.hc[3][:], in0=ctx.ps[bank][:], in1=ctx.hc[2][:],
                                  op=ALU.mult),
                              reads=[f"pb{bank}", "hc2"], writes=["hc3"], tag=f"{uq}nr{hd_i}")
                        pb.op("vector", lambda e, ev=ev: e.tensor_scalar(
                                  out=ev[:], in0=ctx.hc[3][:], scalar1=ctx.attn_const[L]["onw"][:, 0:1],
                                  scalar2=None, op0=ALU.mult),
                              reads=["hc3", "attnc"], writes=[evk], tag=f"{uq}no{hd_i}")
                        row = qh * HD + hd_i * P
                        col = (qc % 4) * TC
                        dstb = A["oS"][qc // 4]
                        pb.dma("gpsimd", out=dstb[row:row + P, col:col + TC], in_=ev[:],
                               reads=[evk], writes=[KOS], tag=f"{uq}so{hd_i}")
                        continue
                    pb.op("vector", lambda e, bank=bank, ev=ev: e.tensor_tensor(
                              out=ev[:], in0=ctx.ps[bank][:], in1=ctx.hc[2][:], op=ALU.mult),
                          reads=[f"pb{bank}", "hc2"], writes=[evk], tag=f"{uq}no{hd_i}")
                    row = qh * HD + hd_i * P
                    col = (qc % 4) * TC
                    dstb = A["oS"][qc // 4]
                    pb.dma("gpsimd", out=dstb[row:row + P, col:col + TC], in_=ev[:],
                           reads=[evk], writes=[KOS], tag=f"{uq}so{hd_i}")


def emit_attention_out(ctx, A, x_in, x_out):
    pb = ctx.pb
    cfg = A["cfg"]
    L = cfg["name"]
    CQ = cfg["nh_local"] * cfg["hd"]
    C2 = 2 * CQ
    NCC = C2 // P
    u0 = A["u0"]
    oS0, oS1 = A["oS"]
    OG0, OG1 = A["OG"]
    KOS, KOR = A["KOS"], A["KOR"]
    pb.collective("AllGather", ALU.bypass, PAIRS, [oS0], [OG0],
                  reads=[KOS], writes=[KOR], tag=u0 + "ag0")
    pb.collective("AllGather", ALU.bypass, PAIRS, [oS1], [OG1],
                  reads=[KOS], writes=[KOR], tag=u0 + "ag1")
    wo_ap = ctx.ins[cfg["wo"]]
    relu_out = cfg.get("relu_out", False)
    for tc in range(NTC):
        t0 = tc * TC
        uq = f"{u0}o{tc}_"
        for j in range(NDC):
            pb.dma("gpsimd", out=ctx.xt[j][:], in_=x_in[j * P:(j + 1) * P, t0:t0 + TC],
                   writes=[f"xt{j}"], tag=f"{uq}lx{j}")
        for dh in range(2):
            ngrp = (NCC + 15) // 16
            for ch in range(ngrp):
                c0 = ch * 16
                ncc = min(16, NCC - c0)
                for cc in range(ncc):
                    pb.dma("sync", out=ctx.xn[cc][:],
                           in_=OG0[(c0 + cc) * P:(c0 + cc + 1) * P, t0:t0 + TC],
                           reads=[KOR], writes=[f"xn{cc}"], tag=f"{uq}lo{dh}_{c0 + cc}")
                    ob = ctx.hb[32 + cc % 8]
                    obk = f"hb{32 + cc % 8}"
                    pb.dma("sync", out=ob[:],
                           in_=OG1[(c0 + cc) * P:(c0 + cc + 1) * P, t0:t0 + TC],
                           reads=[KOR], writes=[obk], tag=f"{uq}lp{dh}_{c0 + cc}")
                    pb.op("vector", lambda e, ob=ob: e.tensor_scalar(
                              out=ob[:], in0=ob[:], scalar1=ctx.msel[:, 1:2],
                              scalar2=None, op0=ALU.mult),
                          reads=[obk, "msel"], writes=[obk], tag=f"{uq}s1{dh}_{c0 + cc}")
                    pb.op("vector", lambda e, cc=cc, ob=ob: e.scalar_tensor_tensor(
                              out=ctx.xn[cc][:], in0=ctx.xn[cc][:], scalar=ctx.msel[:, 0:1],
                              in1=ob[:], op0=ALU.mult, op1=ALU.add),
                          reads=[f"xn{cc}", obk, "msel"], writes=[f"xn{cc}"],
                          tag=f"{uq}s2{dh}_{c0 + cc}")
                    wi = cc % 4
                    pb.dma("sync", out=ctx.wt[wi][:],
                           in_=wo_ap[(c0 + cc) * P:(c0 + cc + 1) * P, dh * 1024:(dh + 1) * 1024],
                           writes=[f"wt{wi}"], tag=f"{uq}lw{dh}_{c0 + cc}")
                    for dt in range(8):
                        pb.op("tensor", lambda e, wi=wi, cc=cc, dt=dt, ch=ch, ncc=ncc: e.matmul(
                                  ctx.ps[dt][:], ctx.wt[wi][:, dt * P:(dt + 1) * P],
                                  ctx.xn[cc][:], start=(ch == 0 and cc == 0),
                                  stop=(ch == ngrp - 1 and cc == ncc - 1)),
                              reads=[f"wt{wi}", f"xn{cc}"], writes=[f"pb{dt}"],
                              tag=f"{uq}mm{dh}_{c0 + cc}_{dt}")
            for dt in range(8):
                j = dh * 8 + dt
                if relu_out:
                    bo = ctx.attn_const[L]["bo"]
                    ev = ctx.hb[56 + dt % 4]
                    evk = f"hb{56 + dt % 4}"
                    pb.op("scalar", lambda e, dt=dt, j=j, ev=ev: e.activation(
                              out=ev[:], in_=ctx.ps[dt][:], func=AF.Relu,
                              bias=bo[:, j:j + 1]),
                          reads=[f"pb{dt}", "attnc"], writes=[evk], tag=f"{uq}rl{j}")
                    pb.op("vector", lambda e, j=j, ev=ev: e.tensor_tensor(
                              out=ctx.xt[j][:], in0=ctx.xt[j][:], in1=ev[:], op=ALU.add),
                          reads=[f"xt{j}", evk], writes=[f"xt{j}"], tag=f"{uq}ra{j}")
                else:
                    pb.op("vector", lambda e, dt=dt, j=j: e.scalar_tensor_tensor(
                              out=ctx.xt[j][:], in0=ctx.ps[dt][:], scalar=1.0,
                              in1=ctx.xt[j][:], op0=ALU.mult, op1=ALU.add),
                          reads=[f"pb{dt}", f"xt{j}"], writes=[f"xt{j}"], tag=f"{uq}ra{j}")
        for j in range(NDC):
            pb.dma("gpsimd", out=x_out[j * P:(j + 1) * P, t0:t0 + TC], in_=ctx.xt[j][:],
                   reads=[f"xt{j}"], writes=["xout"], tag=f"{uq}sx{j}")


def setup_attn_consts(ctx, cfg):
    L = cfg["name"]
    d = {}
    sb = ctx.sb
    pb = ctx.pb
    if cfg.get("rope_d2"):
        d["cos"] = sb("sb_" + L + "_cos", [P, SEQ], BF16)
        d["sin"] = sb("sb_" + L + "_sin", [P, SEQ], BF16)
        if cfg["rope_d2"] == 64 and not hasattr(ctx, "swap64"):
            ctx.swap64 = sb("sb_swap64", [P, P], BF16)
            pb.dma("gpsimd", out=ctx.swap64[:], in_=ctx.ins["c_swap64"], writes=["swap64"])
        pb.dma("gpsimd", out=d["cos"][:], in_=ctx.ins[L + "_cos"], writes=["attnc"])
        pb.dma("gpsimd", out=d["sin"][:], in_=ctx.ins[L + "_sin"], writes=["attnc"])
    if cfg.get("relu_qkv"):
        nq_t = cfg["nh_local"] * cfg["hd"] // P
        for bn in ("bq", "bk"):
            d[bn] = sb("sb_" + L + "_" + bn, [P, nq_t], F32)
            pb.dma("gpsimd", out=d[bn][:], in_=ctx.ins[L + "_" + bn], writes=["attnc"])
        d["bv"] = sb("sb_" + L + "_bv", [1, cfg["nh_local"] * cfg["hd"]], BF16)
        pb.dma("gpsimd", out=d["bv"][:], in_=ctx.ins[L + "_bv"], writes=["attnc"])
    if cfg.get("relu_out"):
        d["bo"] = sb("sb_" + L + "_bo", [P, 16], F32)
        pb.dma("gpsimd", out=d["bo"][:], in_=ctx.ins[L + "_bo"], writes=["attnc"])
    if cfg.get("o_rms"):
        d["onw"] = sb("sb_" + L + "_onw", [P, 1], F32)
        pb.dma("gpsimd", out=d["onw"][:], in_=ctx.ins[L + "_onw"], writes=["attnc"])
    if cfg.get("gating"):
        ctx.l2km = sb("sb_l2km", [P, 16], F32)
        ctx.l2kmb = sb("sb_l2kmb", [P, 16], BF16)
        ctx.l2selT = sb("sb_l2selT", [16, SEQ], BF16)
        ctx.l2mr = sb("sb_l2mr", [1, 16 * TC], BF16)
        ctx.iota16 = sb("sb_iota16", [P, 16], F32)
        ctx.neg_row = sb("sb_negrow", [1, P], BF16)
        ctx.l2gm = sb("sb_l2gm", [P, 16 * (SEQ // P)], F32)
        ctx.l2scr16 = ctx.nc.dram_tensor("l2scr16", [16, TC], BF16, kind="Internal").ap()
        pb.dma("gpsimd", out=ctx.iota16[:], in_=ctx.ins["c_iota16"], writes=["iota16"])
        pb.dma("gpsimd", out=ctx.neg_row[:], in_=ctx.ins["c_neg_row"], writes=["neg_row"])
        pb.dma("gpsimd", out=ctx.l2gm[:], in_=ctx.ins["c_l2gm"], writes=["l2gm"])
        if not hasattr(ctx, "l3_idx"):
            ctx.l3_idx = sb("sb_l3idx", [P, 8], U32)
        if not hasattr(ctx, "ident_f32"):
            ctx.ident_f32 = sb("sb_identf", [P, P], F32)
            pb.dma("gpsimd", out=ctx.ident_f32[:], in_=ctx.ins["c_ident_f32"], writes=["identf"])
    ctx.attn_const[L] = d


ATTN_CFGS = {
    "l1attn": dict(name="l1a", hd=128, nh_local=8, kv_ratio=1, causal=True,
                   scale=float(1.0 / np.sqrt(128.0)),
                   wq="L1wq", wk="L1wk", wv="L1wv", wo="L1wo"),
    "l3attn": dict(name="l3a", hd=1024, nh_local=1, kv_ratio=1, causal=False,
                   scale=float(1.0 / np.sqrt(1024.0)), relu_qkv=True, relu_out=True,
                   cast_out=True,
                   wq="L3wq", wk="L3wk", wv="L3wv", wo="L3wo"),
    "l0attn": dict(name="l0a", hd=256, nh_local=8, kv_ratio=2, causal=True,
                   scale=float(256.0 ** -0.5), softcap=50.0, rope_d2=128,
                   wq="L0wq", wk="L0wk", wv="L0wv", wo="L0wo"),
    "l2attn": dict(name="l2a", hd=128, nh_local=8, kv_ratio=1, causal=True,
                   scale=float(128.0 ** -0.5), rope_d2=64, gating=True, o_rms=True,
                   wq="L2wq", wk="L2wk", wv="L2wv", wo="L2wo"),
}


def emit_attn_sublayer(ctx, x_in, x_out, cfg):
    A = emit_attention(ctx, x_in, x_out, cfg)
    emit_attention_core(ctx, A)
    emit_attention_out(ctx, A, x_in, x_out)


def emit_l3ffn(ctx, x_in, x_out):
    pb = ctx.pb
    u0 = ctx.uniq("f3_")
    E, FF = 8, 1024
    NFT = FF // P
    rw = ctx.ins["L3router"]
    rmsw = ctx.l3_rmsw
    for tc in range(NTC):
        t0 = tc * TC
        uq = f"{u0}t{tc}_"
        emit_ln_chunk(ctx, x_in, t0, tag=f"{uq}ln")
        for j in range(NDC):
            sq = 48 + (j % 8)
            pb.op("scalar", lambda e, j=j, sq=sq: e.activation(
                      out=ctx.hb[sq][:], in_=ctx.xn[j][:], func=AF.Square),
                  reads=[f"xn{j}"], writes=[f"hb{sq}"], tag=f"{uq}rs{j}")
            pb.op("tensor", lambda e, j=j, sq=sq: e.matmul(ctx.ps[6][0:1, :], ctx.ones_col[:],
                                                           ctx.hb[sq][:], start=(j == 0),
                                                           stop=(j == NDC - 1)),
                  reads=[f"hb{sq}", "ones_col"], writes=["pb6"], tag=f"{uq}rm{j}")
        r0 = ctx.rows[0]
        pb.op("vector", lambda e: e.tensor_scalar(out=r0[:], in0=ctx.ps[6][0:1, :],
                                                  scalar1=1.0 / D, scalar2=None, op0=ALU.mult),
              reads=["pb6"], writes=["row0"], tag=uq + "rv")
        pb.op("scalar", lambda e: e.activation(out=r0[:], in_=r0[:], func=AF.Ln,
                                               bias=ctx.eps_row[0:1, 1:2]),
              reads=["row0", "eps_row"], writes=["row0"], tag=uq + "rl")
        pb.op("scalar", lambda e: e.activation(out=r0[:], in_=r0[:], func=AF.Exp, scale=-0.5),
              reads=["row0"], writes=["row0"], tag=uq + "re")
        pb.op("tensor", lambda e: e.matmul(ctx.ps[6][:], ctx.ones_row_f32[:], r0[:],
                                           start=True, stop=True),
              reads=["row0", "ones_row_f32"], writes=["pb6"], tag=uq + "rb")
        pb.op("vector", lambda e: e.tensor_copy(out=ctx.hc[2][:], in_=ctx.ps[6][:]),
              reads=["pb6"], writes=["hc2"], tag=uq + "rc")
        for j in range(NDC):
            pb.op("vector", lambda e, j=j: e.tensor_tensor(out=ctx.hc[3][:], in0=ctx.xn[j][:],
                                                           in1=ctx.hc[2][:], op=ALU.mult),
                  reads=[f"xn{j}", "hc2"], writes=["hc3"], tag=f"{uq}h1{j}")
            pb.op("gpsimd", lambda e, j=j: e.tensor_scalar(out=ctx.xn[j][:], in0=ctx.hc[3][:],
                                                           scalar1=rmsw[:, j:j + 1], scalar2=None,
                                                           op0=ALU.mult),
                  reads=["hc3", "l3rw"], writes=[f"xn{j}"], tag=f"{uq}h2{j}")
        for j in range(NDC):
            wi = j % 4
            pb.dma("sync", out=ctx.wt[wi][:, 0:8], in_=rw[j * P:(j + 1) * P, :],
                   writes=[f"wt{wi}"], tag=f"{uq}rw{j}")
            pb.op("tensor", lambda e, wi=wi, j=j: e.matmul(ctx.ps[5][0:8, :], ctx.wt[wi][:, 0:8],
                                                           ctx.xn[j][:], start=(j == 0),
                                                           stop=(j == NDC - 1)),
                  reads=[f"wt{wi}", f"xn{j}"], writes=["pb5"], tag=f"{uq}rg{j}")
        pb.op("vector", lambda e: e.tensor_copy(out=ctx.hc[3][0:8, :], in_=ctx.ps[5][0:8, :]),
              reads=["pb5"], writes=["hc3"], tag=uq + "le")
        for qt in range(TC // P):
            pb.op("tensor", lambda e, qt=qt: e.transpose(ctx.ps[7][:, 0:8],
                                                         ctx.hc[3][0:8, qt * P:(qt + 1) * P],
                                                         ctx.ident_f32[0:8, 0:8]),
                  reads=["hc3", "identf"], writes=["pb7"], tag=f"{uq}tr{qt}")
            lt = ctx.cols[4]
            pb.op("vector", lambda e: e.tensor_copy(out=lt[:, 0:8], in_=ctx.ps[7][:, 0:8]),
                  reads=["pb7"], writes=["col4"], tag=f"{uq}lt{qt}")
            mx = ctx.cols[5]
            pb.op("vector", lambda e: e.reduce_max(out=mx[:, 0:1], in_=lt[:, 0:8], axis=AX.X),
                  reads=["col4"], writes=["col5"], tag=f"{uq}mx{qt}")
            pb.op("vector", lambda e: e.tensor_scalar(out=mx[:, 1:2], in0=mx[:, 0:1],
                                                      scalar1=-1.0, scalar2=None, op0=ALU.mult),
                  reads=["col5"], writes=["col5"], tag=f"{uq}mn{qt}")
            pb.op("scalar", lambda e: e.activation(out=lt[:, 0:8], in_=lt[:, 0:8], func=AF.Exp,
                                                   bias=mx[:, 1:2]),
                  reads=["col4", "col5"], writes=["col4"], tag=f"{uq}ex{qt}")
            pb.op("vector", lambda e: e.tensor_reduce(out=mx[:, 2:3], in_=lt[:, 0:8],
                                                      axis=AX.X, op=ALU.add),
                  reads=["col4"], writes=["col5"], tag=f"{uq}sm{qt}")
            pb.op("vector", lambda e: e.reciprocal(out=mx[:, 3:4], in_=mx[:, 2:3]),
                  reads=["col5"], writes=["col5"], tag=f"{uq}rc{qt}")
            pb.op("vector", lambda e: e.tensor_scalar(out=lt[:, 0:8], in0=lt[:, 0:8],
                                                      scalar1=mx[:, 3:4], scalar2=None, op0=ALU.mult),
                  reads=["col4", "col5"], writes=["col4"], tag=f"{uq}pr{qt}")
            tv = ctx.cols[6]
            ti = ctx.l3_idx
            pb.op("vector", lambda e: e.max(out=tv[:, 0:8], in_=lt[:, 0:8]),
                  reads=["col4"], writes=["col6"], tag=f"{uq}t8{qt}")
            pb.op("vector", lambda e: e.max_index(out=ti[:], in_max=tv[:, 0:8], in_values=lt[:, 0:8]),
                  reads=["col6", "col4"], writes=["l3idx"], tag=f"{uq}ti{qt}")
            pb.op("vector", lambda e: e.tensor_tensor(out=mx[:, 4:5], in0=tv[:, 0:1],
                                                      in1=tv[:, 1:2], op=ALU.add),
                  reads=["col6"], writes=["col5"], tag=f"{uq}sv{qt}")
            pb.op("vector", lambda e: e.reciprocal(out=mx[:, 5:6], in_=mx[:, 4:5]),
                  reads=["col5"], writes=["col5"], tag=f"{uq}rv{qt}")
            pb.op("vector", lambda e: e.tensor_scalar(out=tv[:, 8:10], in0=tv[:, 0:2],
                                                      scalar1=mx[:, 5:6], scalar2=None, op0=ALU.mult),
                  reads=["col6", "col5"], writes=["col6"], tag=f"{uq}wn{qt}")
            pb.op("vector", lambda e: e.tensor_copy(out=mx[:, 6:7], in_=ti[:, 0:1]),
                  reads=["l3idx"], writes=["col5"], tag=f"{uq}i1{qt}")
            pb.op("vector", lambda e: e.tensor_copy(out=mx[:, 7:8], in_=ti[:, 1:2]),
                  reads=["l3idx"], writes=["col5"], tag=f"{uq}i2{qt}")
            gt = ctx.cols[7]
            pb.op("vector", lambda e: e.tensor_scalar(out=gt[:, 0:8], in0=ctx.iota8[:, 0:8],
                                                      scalar1=mx[:, 6:7], scalar2=None,
                                                      op0=ALU.is_equal),
                  reads=["iota8", "col5"], writes=["col7"], tag=f"{uq}o1{qt}")
            pb.op("vector", lambda e: e.tensor_scalar(out=gt[:, 0:8], in0=gt[:, 0:8],
                                                      scalar1=tv[:, 8:9], scalar2=None, op0=ALU.mult),
                  reads=["col7", "col6"], writes=["col7"], tag=f"{uq}o2{qt}")
            pb.op("vector", lambda e: e.tensor_scalar(out=gt[:, 8:16], in0=ctx.iota8[:, 0:8],
                                                      scalar1=mx[:, 7:8], scalar2=None,
                                                      op0=ALU.is_equal),
                  reads=["iota8", "col5"], writes=["col7"], tag=f"{uq}o3{qt}")
            pb.op("vector", lambda e: e.scalar_tensor_tensor(out=gt[:, 0:8], in0=gt[:, 8:16],
                                                             scalar=tv[:, 9:10], in1=gt[:, 0:8],
                                                             op0=ALU.mult, op1=ALU.add),
                  reads=["col7", "col6"], writes=["col7"], tag=f"{uq}o4{qt}")
            pb.op("vector", lambda e: e.tensor_copy(out=ctx.hc[0][:, 0:8], in_=gt[:, 0:8]),
                  reads=["col7"], writes=["hc0"], tag=f"{uq}gc{qt}")
            pb.op("tensor", lambda e, qt=qt: e.transpose(ctx.ps[7][0:8, 0:P], ctx.hc[0][:, 0:8],
                                                         ctx.ident_f32[:]),
                  reads=["hc0", "identf"], writes=["pb7"], tag=f"{uq}gt{qt}")
            pb.op("vector", lambda e, qt=qt: e.tensor_copy(out=ctx.hc[2][0:8, qt * P:(qt + 1) * P],
                                                           in_=ctx.ps[7][0:8, 0:P]),
                  reads=["pb7"], writes=["hc2"], tag=f"{uq}gs{qt}")
        for ex in range(E):
            ue = f"{uq}e{ex}_"
            pb.dma("gpsimd", out=ctx.rows[2][:], in_=ctx.hc[2][ex:ex + 1, :],
                   reads=["hc2"], writes=["row2"], tag=ue + "gx")
            pb.op("tensor", lambda e: e.matmul(ctx.ps[7][:], ctx.ones_row_f32[:],
                                               ctx.rows[2][:], start=True, stop=True),
                  reads=["row2", "ones_row_f32"], writes=["pb7"], tag=ue + "gb")
            pb.op("vector", lambda e: e.tensor_copy(out=ctx.hc[3][:], in_=ctx.ps[7][:]),
                  reads=["pb7"], writes=["hc3"], tag=ue + "gc")
            for fg in range(2):
                f0 = fg * 4
                for dc in range(NDC):
                    wi = dc % 2
                    pb.dma("sync", out=ctx.wt[wi][:, 0:512],
                           in_=ctx.ins["L3wg"][ex * D + dc * P:ex * D + (dc + 1) * P,
                                               f0 * P:(f0 + 4) * P],
                           writes=[f"wt{wi}"], tag=f"{ue}wg{fg}_{dc}")
                    pb.dma("sync", out=ctx.wt[2 + wi][:, 0:512],
                           in_=ctx.ins["L3wu"][ex * D + dc * P:ex * D + (dc + 1) * P,
                                               f0 * P:(f0 + 4) * P],
                           writes=[f"wt{2 + wi}"], tag=f"{ue}wu{fg}_{dc}")
                    for fi in range(4):
                        pb.op("tensor", lambda e, wi=wi, dc=dc, fi=fi: e.matmul(
                                  ctx.ps[fi][:], ctx.wt[wi][:, fi * P:(fi + 1) * P],
                                  ctx.xn[dc][:], start=(dc == 0), stop=(dc == NDC - 1)),
                              reads=[f"wt{wi}", f"xn{dc}"], writes=[f"pb{fi}"],
                              tag=f"{ue}mg{fg}_{dc}_{fi}")
                        pb.op("tensor", lambda e, wi=wi, dc=dc, fi=fi: e.matmul(
                                  ctx.ps[4 + fi][:], ctx.wt[2 + wi][:, fi * P:(fi + 1) * P],
                                  ctx.xn[dc][:], start=(dc == 0), stop=(dc == NDC - 1)),
                              reads=[f"wt{2 + wi}", f"xn{dc}"], writes=[f"pb{4 + fi}"],
                              tag=f"{ue}mu{fg}_{dc}_{fi}")
                for fi in range(4):
                    ft = f0 + fi
                    pb.op("scalar", lambda e, fi=fi: e.activation(out=ctx.hc[0][:],
                                                                  in_=ctx.ps[fi][:], func=AF.Silu),
                          reads=[f"pb{fi}"], writes=["hc0"], tag=f"{ue}si{ft}")
                    pb.op("vector", lambda e, fi=fi: e.tensor_tensor(out=ctx.hc[1][:],
                                                                     in0=ctx.hc[0][:],
                                                                     in1=ctx.ps[4 + fi][:], op=ALU.mult),
                          reads=["hc0", f"pb{4 + fi}"], writes=["hc1"], tag=f"{ue}su{ft}")
                    hbi = ex * NFT + ft
                    pb.op("vector", lambda e, hbi=hbi: e.tensor_tensor(out=ctx.hb[hbi][:],
                                                                       in0=ctx.hc[1][:], in1=ctx.hc[3][:],
                                                                       op=ALU.mult),
                          reads=["hc1", "hc3"], writes=[f"hb{hbi}"], tag=f"{ue}sw{ft}")
        for dh in range(2):
            for ex in range(E):
                for fc in range(NFT):
                    wi = fc % 4
                    pb.dma("sync", out=ctx.wt[wi][:, 0:1024],
                           in_=ctx.ins["L3wd"][ex * FF + fc * P:ex * FF + (fc + 1) * P,
                                               dh * 1024:(dh + 1) * 1024],
                           writes=[f"wt{wi}"], tag=f"{uq}wd{dh}_{ex}_{fc}")
                    for dt in range(8):
                        pb.op("tensor", lambda e, wi=wi, ex=ex, fc=fc, dt=dt: e.matmul(
                                  ctx.ps[dt][:], ctx.wt[wi][:, dt * P:(dt + 1) * P],
                                  ctx.hb[ex * NFT + fc][:], start=(ex == 0 and fc == 0),
                                  stop=(ex == E - 1 and fc == NFT - 1)),
                              reads=[f"wt{wi}", f"hb{ex * NFT + fc}"], writes=[f"pb{dt}"],
                              tag=f"{uq}md{dh}_{ex}_{fc}_{dt}")
            for dt in range(8):
                j = dh * 8 + dt
                pb.dma("gpsimd", out=ctx.xt[j][:], in_=x_in[j * P:(j + 1) * P, t0:t0 + TC],
                       writes=[f"xt{j}"], tag=f"{uq}lr{j}")
                pb.op("vector", lambda e, dt=dt, j=j: e.scalar_tensor_tensor(
                          out=ctx.xt[j][:], in0=ctx.ps[dt][:], scalar=0.5, in1=ctx.xt[j][:],
                          op0=ALU.mult, op1=ALU.add),
                      reads=[f"pb{dt}", f"xt{j}"], writes=[f"xt{j}"], tag=f"{uq}ra{j}")
        for ft in range(NFT):
            for dc in range(NDC):
                wi = dc % 4
                pb.dma("sync", out=ctx.wt[wi][:, 0:P],
                       in_=ctx.ins["L3swg"][dc * P:(dc + 1) * P, ft * P:(ft + 1) * P],
                       writes=[f"wt{wi}"], tag=f"{uq}sg{ft}_{dc}")
                pb.dma("sync", out=ctx.wt[4 + wi][:, 0:P],
                       in_=ctx.ins["L3swu"][dc * P:(dc + 1) * P, ft * P:(ft + 1) * P],
                       writes=[f"wt{4 + wi}"], tag=f"{uq}suw{ft}_{dc}")
                pb.op("tensor", lambda e, wi=wi, dc=dc: e.matmul(
                          ctx.ps[0][:], ctx.wt[wi][:, 0:P], ctx.xn[dc][:],
                          start=(dc == 0), stop=(dc == NDC - 1)),
                      reads=[f"wt{wi}", f"xn{dc}"], writes=["pb0"], tag=f"{uq}sg{ft}_{dc}m")
                pb.op("tensor", lambda e, wi=wi, dc=dc: e.matmul(
                          ctx.ps[1][:], ctx.wt[4 + wi][:, 0:P], ctx.xn[dc][:],
                          start=(dc == 0), stop=(dc == NDC - 1)),
                      reads=[f"wt{4 + wi}", f"xn{dc}"], writes=["pb1"], tag=f"{uq}su{ft}_{dc}m")
            pb.op("scalar", lambda e: e.activation(out=ctx.hc[0][:], in_=ctx.ps[0][:], func=AF.Silu),
                  reads=["pb0"], writes=["hc0"], tag=f"{uq}ss{ft}")
            pb.op("vector", lambda e, ft=ft: e.tensor_tensor(out=ctx.hb[ft][:], in0=ctx.hc[0][:],
                                                             in1=ctx.ps[1][:], op=ALU.mult),
                  reads=["hc0", "pb1"], writes=[f"hb{ft}"], tag=f"{uq}sm{ft}")
        for dh in range(2):
            for fc in range(NFT):
                wi = fc % 4
                pb.dma("sync", out=ctx.wt[wi][:, 0:1024],
                       in_=ctx.ins["L3swd"][fc * P:(fc + 1) * P, dh * 1024:(dh + 1) * 1024],
                       writes=[f"wt{wi}"], tag=f"{uq}sd{dh}_{fc}")
                for dt in range(8):
                    pb.op("tensor", lambda e, wi=wi, fc=fc, dt=dt: e.matmul(
                              ctx.ps[dt][:], ctx.wt[wi][:, dt * P:(dt + 1) * P],
                              ctx.hb[fc][:], start=(fc == 0), stop=(fc == NFT - 1)),
                          reads=[f"wt{wi}", f"hb{fc}"], writes=[f"pb{dt}"],
                          tag=f"{uq}ms{dh}_{fc}_{dt}")
            for dt in range(8):
                j = dh * 8 + dt
                pb.op("vector", lambda e, dt=dt, j=j: e.scalar_tensor_tensor(
                          out=ctx.xt[j][:], in0=ctx.ps[dt][:], scalar=0.5, in1=ctx.xt[j][:],
                          op0=ALU.mult, op1=ALU.add),
                      reads=[f"pb{dt}", f"xt{j}"], writes=[f"xt{j}"], tag=f"{uq}rb{j}")
                pb.dma("gpsimd", out=x_out[j * P:(j + 1) * P, t0:t0 + TC], in_=ctx.xt[j][:],
                       reads=[f"xt{j}"], writes=["xout"], tag=f"{uq}so{j}")


def setup_l3ffn_consts(ctx):
    ctx.l3_rmsw = ctx.sb("sb_l3rw", [P, 16], F32)
    ctx.pb.dma("gpsimd", out=ctx.l3_rmsw[:], in_=ctx.ins["L3rmsw"], writes=["l3rw"])
    ctx.l3_idx = ctx.sb("sb_l3idx", [P, 8], U32)
    ctx.iota8 = ctx.sb("sb_iota8", [P, 8], F32)
    ctx.pb.dma("gpsimd", out=ctx.iota8[:], in_=ctx.ins["c_iota8"], writes=["iota8"])
    if not hasattr(ctx, "ident_f32"):
        ctx.ident_f32 = ctx.sb("sb_identf", [P, P], F32)
        ctx.pb.dma("gpsimd", out=ctx.ident_f32[:], in_=ctx.ins["c_ident_f32"], writes=["identf"])


def emit_l2ffn(ctx, x_in, x_out):
    pb = ctx.pb
    u0 = ctx.uniq("f2_")
    FF = 8192
    NFC = FF // P
    s2k = float(1.0 / np.sqrt(2048.0))
    s8k = float(1.0 / np.sqrt(8192.0))
    if not hasattr(ctx, "l2_scales_done"):
        ctx.l2_scales_done = True
        ctx.l2_us = ctx.sb("sb_l2us", [P, 64], F32)
        ctx.l2_gsu = ctx.sb("sb_l2gs", [P, 64], F32)
        ctx.l2_suus = ctx.sb("sb_l2suus", [P, 64], F32)
        ctx.l2_ds = ctx.sb("sb_l2ds", [P, 16], F32)
        ctx.l2_sug = ctx.sb("sb_l2sug", [P, 16], F32)
        tmp = ctx.sb("sb_l2tmp", [P, 64], F32)
        pb.dma("gpsimd", out=ctx.l2_sug[:], in_=ctx.ins["L2suu"], writes=["l2sug"])
        pb.dma("gpsimd", out=tmp[:], in_=ctx.ins["L2wsu"], writes=["l2tmp"])
        pb.op("vector", lambda e: e.tensor_scalar(out=ctx.l2_us[:], in0=tmp[:], scalar1=s2k,
                                                  scalar2=None, op0=ALU.mult),
              reads=["l2tmp"], writes=["l2us"], tag=u0 + "us")
        pb.dma("gpsimd", out=tmp[:], in_=ctx.ins["L2wsg"], writes=["l2tmp"])
        pb.op("vector", lambda e: e.tensor_scalar(out=ctx.l2_gsu[:], in0=tmp[:], scalar1=s2k,
                                                  scalar2=None, op0=ALU.mult),
              reads=["l2tmp"], writes=["l2gs"], tag=u0 + "gs")
        pb.dma("gpsimd", out=tmp[:], in_=ctx.ins["L2sud"], writes=["l2tmp"])
        pb.op("vector", lambda e: e.tensor_tensor(out=ctx.l2_suus[:], in0=tmp[:],
                                                  in1=ctx.l2_us[:], op=ALU.mult),
              reads=["l2tmp", "l2us"], writes=["l2suus"], tag=u0 + "suus")
        t2 = ctx.sb("sb_l2t2", [P, 16], F32)
        pb.dma("gpsimd", out=t2[:], in_=ctx.ins["L2wsd"], writes=["l2t2"])
        pb.op("vector", lambda e: e.tensor_scalar(out=ctx.l2_ds[:], in0=t2[:], scalar1=0.5 * s8k,
                                                  scalar2=None, op0=ALU.mult),
              reads=["l2t2"], writes=["l2ds"], tag=u0 + "ds")

    def h16_stages(uq, pool, keys, nch, nstage):
        for st in range(nstage):
            hh = 1 << st
            for base in range(0, nch, 2 * hh):
                for off in range(hh):
                    a, b = base + off, base + off + hh
                    ka, kb = keys(a), keys(b)
                    pb.op("vector", lambda e, a=a, b=b: e.tensor_tensor(
                              out=ctx.hc[0][:], in0=pool(a)[:], in1=pool(b)[:], op=ALU.subtract),
                          reads=[ka, kb], writes=["hc0"], tag=f"{uq}bs{st}_{a}")
                    pb.op("gpsimd", lambda e, a=a, b=b: e.tensor_tensor(
                              out=pool(a)[:], in0=pool(a)[:], in1=pool(b)[:], op=ALU.add),
                          reads=[ka, kb], writes=[ka], tag=f"{uq}ba{st}_{a}")
                    pb.op("vector", lambda e, b=b: e.tensor_copy(out=pool(b)[:], in_=ctx.hc[0][:]),
                          reads=["hc0"], writes=[kb], tag=f"{uq}bc{st}_{a}")

    for tc in range(NTC):
        t0 = tc * TC
        uq = f"{u0}t{tc}_"
        emit_ln_chunk(ctx, x_in, t0, tag=f"{uq}ln")
        for j in range(NDC):
            pb.op("vector", lambda e, j=j: e.tensor_scalar(out=ctx.xn[j][:], in0=ctx.xn[j][:],
                                                           scalar1=ctx.l2_sug[:, j:j + 1],
                                                           scalar2=None, op0=ALU.mult),
                  reads=[f"xn{j}", "l2sug"], writes=[f"xn{j}"], tag=f"{uq}su{j}")
        for j in range(NDC):
            bank = j % 2
            pb.op("tensor", lambda e, j=j, bank=bank: e.matmul(
                      ctx.ps[bank][:], ctx.h128b[:], ctx.xn[j][:], start=True, stop=True),
                  reads=[f"xn{j}", "h128b"], writes=[f"pb{bank}"], tag=f"{uq}hm{j}")
            pb.op("vector", lambda e, j=j, bank=bank: e.tensor_copy(out=ctx.xt[j][:],
                                                                    in_=ctx.ps[bank][:]),
                  reads=[f"pb{bank}"], writes=[f"xt{j}"], tag=f"{uq}he{j}")
        h16_stages(uq + "a", lambda a: ctx.xt[a], lambda a: f"xt{a}", NDC, 4)
        for j in range(NDC):
            pb.op("scalar", lambda e, j=j: e.copy(out=ctx.xn[j][:], in_=ctx.xt[j][:]),
                  reads=[f"xt{j}"], writes=[f"xn{j}"], tag=f"{uq}cb{j}")
        for fg in range(16):
            f0 = fg * 4
            for dc in range(NDC):
                wi = dc % 2
                pb.dma("sync", out=ctx.wt[wi][:, 0:512],
                       in_=ctx.ins["L2wg"][dc * P:(dc + 1) * P, f0 * P:(f0 + 4) * P],
                       writes=[f"wt{wi}"], tag=f"{uq}wg{fg}_{dc}")
                pb.dma("sync", out=ctx.wt[2 + wi][:, 0:512],
                       in_=ctx.ins["L2wu"][dc * P:(dc + 1) * P, f0 * P:(f0 + 4) * P],
                       writes=[f"wt{2 + wi}"], tag=f"{uq}wu{fg}_{dc}")
                for fi in range(4):
                    pb.op("tensor", lambda e, wi=wi, dc=dc, fi=fi: e.matmul(
                              ctx.ps[fi][:], ctx.wt[wi][:, fi * P:(fi + 1) * P],
                              ctx.xn[dc][:], start=(dc == 0), stop=(dc == NDC - 1)),
                          reads=[f"wt{wi}", f"xn{dc}"], writes=[f"pb{fi}"],
                          tag=f"{uq}mg{fg}_{dc}_{fi}")
                    pb.op("tensor", lambda e, wi=wi, dc=dc, fi=fi: e.matmul(
                              ctx.ps[4 + fi][:], ctx.wt[2 + wi][:, fi * P:(fi + 1) * P],
                              ctx.xn[dc][:], start=(dc == 0), stop=(dc == NDC - 1)),
                          reads=[f"wt{2 + wi}", f"xn{dc}"], writes=[f"pb{4 + fi}"],
                          tag=f"{uq}mu{fg}_{dc}_{fi}")
            for fi in range(4):
                fc = f0 + fi
                pb.op("vector", lambda e, fi=fi, fc=fc: e.tensor_scalar(
                          out=ctx.hc[0][:], in0=ctx.ps[fi][:], scalar1=ctx.l2_gsu[:, fc:fc + 1],
                          scalar2=None, op0=ALU.mult),
                      reads=[f"pb{fi}", "l2gs"], writes=["hc0"], tag=f"{uq}gm{fc}")
                pb.op("scalar", lambda e: e.activation(out=ctx.hc[1][:], in_=ctx.hc[0][:],
                                                       func=AF.Silu),
                      reads=["hc0"], writes=["hc1"], tag=f"{uq}sl{fc}")
                pb.op("vector", lambda e, fi=fi, fc=fc: e.scalar_tensor_tensor(
                          out=ctx.hb[fc].bitcast(F16)[:], in0=ctx.hc[1][:],
                          scalar=ctx.l2_suus[:, fc:fc + 1], in1=ctx.ps[4 + fi][:],
                          op0=ALU.mult, op1=ALU.mult),
                      reads=["hc1", "l2suus", f"pb{4 + fi}"], writes=[f"hb{fc}"],
                      tag=f"{uq}sw{fc}")
        for fc in range(NFC):
            bank = fc % 2
            pb.op("tensor", lambda e, fc=fc, bank=bank: e.matmul(
                      ctx.ps[bank][:], ctx.h128h[:], ctx.hb[fc].bitcast(F16)[:],
                      start=True, stop=True),
                  reads=[f"hb{fc}", "h128h"], writes=[f"pb{bank}"], tag=f"{uq}mh{fc}")
            pb.op("vector", lambda e, fc=fc, bank=bank: e.tensor_copy(
                      out=ctx.hb[fc].bitcast(F16)[:], in_=ctx.ps[bank][:]),
                  reads=[f"pb{bank}"], writes=[f"hb{fc}"], tag=f"{uq}me{fc}")
        h16_stages(uq + "b", lambda a: ctx.hb[a].bitcast(F16), lambda a: f"hb{a}", NFC, 6)
        for dh in range(2):
            for fc in range(NFC):
                wi = fc % 4
                pb.dma("sync", out=ctx.wt[wi].bitcast(F16)[:, 0:1024],
                       in_=ctx.ins["L2wd"][fc * P:(fc + 1) * P, dh * 1024:(dh + 1) * 1024],
                       writes=[f"wt{wi}"], tag=f"{uq}wd{dh}_{fc}")
                for dt in range(8):
                    pb.op("tensor", lambda e, wi=wi, fc=fc, dt=dt: e.matmul(
                              ctx.ps[dt][:], ctx.wt[wi].bitcast(F16)[:, dt * P:(dt + 1) * P],
                              ctx.hb[fc].bitcast(F16)[:], start=(fc == 0), stop=(fc == NFC - 1)),
                          reads=[f"wt{wi}", f"hb{fc}"], writes=[f"pb{dt}"],
                          tag=f"{uq}md{dh}_{fc}_{dt}")
            for dt in range(8):
                j = dh * 8 + dt
                pb.dma("gpsimd", out=ctx.xt[j][:], in_=x_in[j * P:(j + 1) * P, t0:t0 + TC],
                       writes=[f"xt{j}"], tag=f"{uq}lr{j}")
                pb.op("vector", lambda e, dt=dt, j=j: e.scalar_tensor_tensor(
                          out=ctx.xt[j][:], in0=ctx.ps[dt][:], scalar=ctx.l2_ds[:, j:j + 1],
                          in1=ctx.xt[j][:], op0=ALU.mult, op1=ALU.add),
                      reads=[f"pb{dt}", "l2ds", f"xt{j}"], writes=[f"xt{j}"], tag=f"{uq}ra{j}")
                pb.dma("gpsimd", out=x_out[j * P:(j + 1) * P, t0:t0 + TC], in_=ctx.xt[j][:],
                       reads=[f"xt{j}"], writes=["xout"], tag=f"{uq}so{j}")


def setup_l2ffn_consts(ctx):
    ctx.h128b = ctx.sb("sb_h128b", [P, P], BF16)
    ctx.pb.dma("gpsimd", out=ctx.h128b[:], in_=ctx.ins["c_h128_bf"], writes=["h128b"])
    ctx.h128h = ctx.sb("sb_h128h", [P, P], F16)
    ctx.pb.dma("gpsimd", out=ctx.h128h[:], in_=ctx.ins["c_h128_f16"], writes=["h128h"])


def setup_l0ffn_consts(ctx):
    if hasattr(ctx, "l0_done"):
        return
    ctx.l0_done = True
    sb = ctx.sb
    pb = ctx.pb
    nc = ctx.nc
    if not hasattr(ctx, "ident_f32"):
        ctx.ident_f32 = sb("sb_identf", [P, P], F32)
        pb.dma("gpsimd", out=ctx.ident_f32[:], in_=ctx.ins["c_ident_f32"], writes=["identf"])
    ctx.l0_gam = sb("sb_l0gam", [P, 16], F32)
    ctx.l0_bet = sb("sb_l0bet", [P, 16], F32)
    pb.dma("gpsimd", out=ctx.l0_gam[:], in_=ctx.ins["L0gam"], writes=["l0gam"])
    pb.dma("gpsimd", out=ctx.l0_bet[:], in_=ctx.ins["L0bet"], writes=["l0bet"])
    ctx.l0_tbl = sb("sb_l0tbl", [64, 1040], F32)
    pb.dma("gpsimd", out=ctx.l0_tbl[:], in_=ctx.ins["L0tbl"], writes=["l0tbl"])
    ctx.l0_misc = sb("sb_l0misc", [P, 64 * 5], F32)
    pb.dma("gpsimd", out=ctx.l0_misc[:], in_=ctx.ins["c_l0misc"], writes=["l0misc"])
    ctx.l0_g16 = sb("sb_l0g16", [P, 512], F32)
    pb.dma("gpsimd", out=ctx.l0_g16[:], in_=ctx.ins["c_l0g16"], writes=["l0g16"])
    ctx.l0_exp = sb("sb_l0exp", [64, D], BF16)
    pb.dma("gpsimd", out=ctx.l0_exp[:], in_=ctx.ins["c_l0exp"], writes=["l0exp"])
    ctx.l0_fold = sb("sb_l0fold", [P, 32], F32)
    pb.dma("gpsimd", out=ctx.l0_fold[:], in_=ctx.ins["c_l0fold"], writes=["l0fold"])
    ctx.l0_rep = sb("sb_l0rep", [32, P], F32)
    pb.dma("gpsimd", out=ctx.l0_rep[:], in_=ctx.ins["c_l0rep"], writes=["l0rep"])
    ctx.l0_asn = sb("sb_l0asn", [64, 8], BF16)
    pb.dma("gpsimd", out=ctx.l0_asn[:], in_=ctx.ins["c_l0asn"], writes=["l0asn"])
    ctx.l0_xs = sb("sb_l0xs", [P, 512], F32)
    ctx.l0_ohf = sb("sb_l0ohf", [P, 1024], F32)
    ctx.l0_ohtb = sb("sb_l0ohtb", [64, D], BF16)
    ctx.l0_tm = sb("sb_l0tm", [64, 4], F32)
    ctx.l0_u16 = sb("sb_l0u16", [P, 512], F32)
    ctx.l0_sg = [sb(f"sb_l0sg{dc}", [P, 64], BF16) for dc in range(NDC)]
    ctx.l0_tsc = sb("sb_l0tsc", [P, 64], F32)
    ctx.l0_csc = sb("sb_l0csc", [P, 8], F32)
    ctx.l0_gmm = sb("sb_l0gmm", [P, 128], F32)
    ctx.l0_ohtbf = sb("sb_l0ohtbf", [64, D], F32)
    ctx.l0_bs = sb("sb_l0bs", [P, 128], F32)
    ctx.l0_dT = sb("sb_l0dT", [32, TC], F32)
    ctx.l0_ssg = sb("sb_l0ssg", [64, 32], F32)
    ctx.l0_ssd = nc.dram_tensor("l0ssd", [64, 32], F32, kind="Internal").ap()
    ctx.l0_tmd = nc.dram_tensor("l0tmd", [64, 4], F32, kind="Internal").ap()
    ctx.l0_tmg = nc.dram_tensor("l0tmg", [8 * 64, 4], F32, kind="Internal").ap()

    u = "l0s_"
    slp = ctx.rows[0]
    pb.op("vector", lambda e: e.tensor_reduce(
              out=ctx.l0_ssg[:, 0:32],
              in_=ctx.l0_tbl[0:64, 512:1024].rearrange("p (j g) -> p j g", g=16),
              axis=AX.X, op=ALU.add),
          reads=["l0tbl"], writes=["l0ssg"], tag=u + "mr")
    pb.op("scalar", lambda e: e.activation(out=ctx.l0_ssg[:], in_=ctx.l0_ssg[:],
                                           func=AF.Sign),
          reads=["l0ssg"], writes=["l0ssg"], tag=u + "sg")
    pb.dma("gpsimd", out=ctx.l0_ssd, in_=ctx.l0_ssg[:], reads=["l0ssg"], writes=["l0ssd"],
           tag=u + "sd")
    for dc in range(NDC):
        pb.op("vector", lambda e, dc=dc: e.memset(ctx.l0_sg[dc][:], 0.0),
              writes=[f"l0sg{dc}"], tag=f"{u}z{dc}")
        cv = ctx.cols[0]
        pb.dma("gpsimd", out=cv[:, 0:1],
               in_=ctx.l0_ssd[4 * dc:4 * dc + 4, :].rearrange("a b -> (a b)").unsqueeze(1),
               reads=["l0ssd"], writes=["col0"], tag=f"{u}cv{dc}")
        for c4 in range(4):
            t64 = 4 * dc + c4
            pb.op("vector", lambda e, c4=c4, t64=t64, dc=dc: e.tensor_scalar(
                      out=ctx.l0_sg[dc][:, t64:t64 + 1], in0=cv[:, 0:1],
                      scalar1=ctx.l0_misc[:, 192 + c4:193 + c4], scalar2=None, op0=ALU.mult),
                  reads=["col0", "l0misc"], writes=[f"l0sg{dc}"], tag=f"{u}p{dc}_{c4}")


def emit_l0ffn(ctx, x_in, x_out):
    pb = ctx.pb
    u0 = ctx.uniq("f0_")
    setup_l0ffn_consts(ctx)
    M = ctx.l0_misc
    GRP = [[list(range(8))]]
    pb.op("vector", lambda e: e.memset(ctx.l0_tm[:, 0:1], 1e30), writes=["l0tm"], tag=u0 + "zi")
    pb.op("gpsimd", lambda e: e.memset(ctx.l0_tm[:, 1:2], -1e30), writes=["l0tm"], tag=u0 + "za")
    pb.op("vector", lambda e: e.memset(ctx.l0_tm[:, 2:4], 0.0), writes=["l0tm"], tag=u0 + "zc")

    for tc in range(NTC):
        t0 = tc * TC
        uq = f"{u0}a{tc}_"
        emit_ln_chunk(ctx, x_in, t0, tag=f"{uq}ln")
        for j in range(NDC):
            pb.op("gpsimd", lambda e, j=j, tc=tc: e.tensor_copy(out=ctx.hb[tc * 16 + j][:],
                                                                in_=ctx.xn[j][:]),
                  reads=[f"xn{j}"], writes=[f"hb{tc * 16 + j}"], tag=f"{uq}st{j}")
        for j in range(NDC):
            wtt = ctx.hc[2 + j % 2]
            wk = f"hc{2 + j % 2}"
            pb.op("scalar", lambda e, j=j, wtt=wtt: e.activation(out=wtt[:],
                                                                 in_=ctx.xt[j][:], func=AF.Square),
                  reads=[f"xt{j}"], writes=[wk], tag=f"{uq}sq{j}")
            pb.op("tensor", lambda e, j=j: e.matmul(ctx.ps[6][0:1, :], ctx.ones_col_f32[:],
                                                    ctx.xt[j][:], start=(j == 0), stop=(j == NDC - 1)),
                  reads=[f"xt{j}", "ones_col_f32"], writes=["pb6"], tag=f"{uq}s1_{j}")
            pb.op("tensor", lambda e, j=j, wtt=wtt: e.matmul(ctx.ps[7][0:1, :], ctx.ones_col_f32[:],
                                                             wtt[:], start=(j == 0),
                                                             stop=(j == NDC - 1)),
                  reads=[wk, "ones_col_f32"], writes=["pb7"], tag=f"{uq}s2_{j}")
        r_m, r_ms, r_v, r_r, r_mr = (ctx.rows[k] for k in range(5))
        pb.op("vector", lambda e: e.tensor_scalar(out=r_m[:], in0=ctx.ps[6][0:1, :],
                                                  scalar1=1.0 / D, scalar2=None, op0=ALU.mult),
              reads=["pb6"], writes=["row0"], tag=uq + "m")
        pb.op("vector", lambda e: e.tensor_scalar(out=r_ms[:], in0=ctx.ps[7][0:1, :],
                                                  scalar1=1.0 / D, scalar2=None, op0=ALU.mult),
              reads=["pb7"], writes=["row1"], tag=uq + "ms")
        pb.op("scalar", lambda e: e.activation(out=r_v[:], in_=r_m[:], func=AF.Square),
              reads=["row0"], writes=["row2"], tag=uq + "m2")
        pb.op("vector", lambda e: e.tensor_tensor(out=r_v[:], in0=r_ms[:], in1=r_v[:],
                                                  op=ALU.subtract),
              reads=["row1", "row2"], writes=["row2"], tag=uq + "v")
        pb.op("scalar", lambda e: e.activation(out=r_r[:], in_=r_v[:], func=AF.Ln,
                                               bias=ctx.eps_row[0:1, 0:1]),
              reads=["row2", "eps_row"], writes=["row3"], tag=uq + "lg")
        pb.op("scalar", lambda e: e.activation(out=r_r[:], in_=r_r[:], func=AF.Exp, scale=-0.5),
              reads=["row3"], writes=["row3"], tag=uq + "ex")
        pb.op("vector", lambda e: e.tensor_tensor(out=r_mr[:], in0=r_m[:], in1=r_r[:],
                                                  op=ALU.mult),
              reads=["row0", "row3"], writes=["row4"], tag=uq + "mr")
        pb.op("tensor", lambda e: e.matmul(ctx.ps[6][:], ctx.ones_row_f32[:], r_r[:],
                                           start=True, stop=True),
              reads=["row3", "ones_row_f32"], writes=["pb6"], tag=uq + "br")
        pb.op("tensor", lambda e: e.matmul(ctx.ps[7][:], ctx.ones_row_f32[:], r_mr[:],
                                           start=True, stop=True),
              reads=["row4", "ones_row_f32"], writes=["pb7"], tag=uq + "bm")
        pb.op("vector", lambda e: e.tensor_copy(out=ctx.hc[2][:], in_=ctx.ps[6][:]),
              reads=["pb6"], writes=["hc2"], tag=uq + "cb")
        pb.op("vector", lambda e: e.tensor_copy(out=ctx.hc[3][:], in_=ctx.ps[7][:]),
              reads=["pb7"], writes=["hc3"], tag=uq + "cm")
        for j in range(NDC):
            pb.op("vector", lambda e, j=j: e.tensor_tensor(out=ctx.hc[0][:], in0=ctx.xt[j][:],
                                                           in1=ctx.hc[2][:], op=ALU.mult),
                  reads=[f"xt{j}", "hc2"], writes=["hc0"], tag=f"{uq}n1_{j}")
            pb.op("vector", lambda e, j=j: e.tensor_tensor(out=ctx.hc[1][:], in0=ctx.hc[0][:],
                                                           in1=ctx.hc[3][:], op=ALU.subtract),
                  reads=["hc0", "hc3"], writes=["hc1"], tag=f"{uq}n2_{j}")
            pb.op("vector", lambda e, j=j: e.tensor_scalar(out=ctx.xt[j][:], in0=ctx.hc[1][:],
                                                           scalar1=ctx.l0_gam[:, j:j + 1],
                                                           scalar2=ctx.l0_bet[:, j:j + 1],
                                                           op0=ALU.mult, op1=ALU.add),
                  reads=["hc1", "l0gam", "l0bet"], writes=[f"xt{j}"], tag=f"{uq}n3_{j}")
            pb.op("scalar", lambda e, j=j: e.activation(out=ctx.xn[j][:], in_=ctx.xt[j][:],
                                                        func=AF.Sign),
                  reads=[f"xt{j}"], writes=[f"xn{j}"], tag=f"{uq}sg{j}")
        for j in range(NDC):
            pb.op("tensor", lambda e, j=j: e.matmul(ctx.ps[0][0:64, :], ctx.l0_sg[j][:],
                                                    ctx.xn[j][:], start=(j == 0),
                                                    stop=(j == NDC - 1)),
                  reads=[f"l0sg{j}", f"xn{j}"], writes=["pb0"], tag=f"{uq}ts{j}")
        pb.op("vector", lambda e: e.tensor_copy(out=ctx.hc[0][0:64, :], in_=ctx.ps[0][0:64, :]),
              reads=["pb0"], writes=["hc0"], tag=uq + "tf")
        tsb = ctx.wt[4]
        pb.op("scalar", lambda e: e.copy(out=tsb[0:64, 0:TC], in_=ctx.ps[0][0:64, :]),
              reads=["pb0"], writes=["wt4"], tag=uq + "te")
        pb.op("tensor", lambda e: e.matmul(ctx.ps[1][0:8, :], ctx.l0_asn[:], tsb[0:64, 0:TC],
                                           start=True, stop=True),
              reads=["l0asn", "wt4"], writes=["pb1"], tag=uq + "cs")
        pb.op("vector", lambda e: e.tensor_copy(out=ctx.hc[1][0:8, :], in_=ctx.ps[1][0:8, :]),
              reads=["pb1"], writes=["hc1"], tag=uq + "ce")
        for st in range(4):
            us = f"{uq}s{st}_"
            q0 = st * P
            pb.op("tensor", lambda e, q0=q0: e.transpose(ctx.ps[4][:, 0:64],
                                                         ctx.hc[0][0:64, q0:q0 + P],
                                                         ctx.ident_f32[0:64, 0:64]),
                  reads=["hc0", "identf"], writes=["pb4"], tag=us + "tt")
            tsc = ctx.l0_tsc
            pb.op("vector", lambda e: e.tensor_copy(out=tsc[:], in_=ctx.ps[4][:, 0:64]),
                  reads=["pb4"], writes=["l0tsc"], tag=us + "tc")
            pb.op("tensor", lambda e, q0=q0: e.transpose(ctx.ps[4][:, 0:8],
                                                         ctx.hc[1][0:8, q0:q0 + P],
                                                         ctx.ident_f32[0:8, 0:8]),
                  reads=["hc1", "identf"], writes=["pb4"], tag=us + "ct")
            csc = ctx.l0_csc
            pb.op("vector", lambda e: e.tensor_copy(out=csc[:], in_=ctx.ps[4][:, 0:8]),
                  reads=["pb4"], writes=["l0csc"], tag=us + "cc")
            c0, c1 = ctx.cols[0], ctx.cols[1]
            pb.op("vector", lambda e: e.reduce_max(out=c0[:, 0:1], in_=csc[:], axis=AX.X),
                  reads=["l0csc"], writes=["col0"], tag=us + "bm")
            pb.op("vector", lambda e: e.tensor_scalar(out=ctx.hc[0][:, 0:8], in0=csc[:],
                                                      scalar1=c0[:, 0:1], scalar2=None,
                                                      op0=ALU.is_equal),
                  reads=["l0csc", "col0"], writes=["hc0"], tag=us + "be")
            pb.op("vector", lambda e: e.tensor_tensor(out=ctx.hc[0][:, 0:8], in0=ctx.hc[0][:, 0:8],
                                                      in1=M[:, 256:264], op=ALU.mult),
                  reads=["hc0", "l0misc"], writes=["hc0"], tag=us + "bc")
            pb.op("vector", lambda e: e.tensor_scalar(out=ctx.hc[0][:, 0:8], in0=ctx.hc[0][:, 0:8],
                                                      scalar1=-1.0, scalar2=99.0,
                                                      op0=ALU.mult, op1=ALU.add),
                  reads=["hc0"], writes=["hc0"], tag=us + "bd")
            pb.op("vector", lambda e: e.tensor_reduce(out=c0[:, 1:2], in_=ctx.hc[0][:, 0:8],
                                                      axis=AX.X, op=ALU.min),
                  reads=["hc0"], writes=["col0"], tag=us + "bi")
            pb.op("vector", lambda e: e.tensor_scalar(out=ctx.hc[0][:, 0:64], in0=M[:, 128:192],
                                                      scalar1=c0[:, 1:2], scalar2=None,
                                                      op0=ALU.is_equal),
                  reads=["l0misc", "col0"], writes=["hc0"], tag=us + "ic")
            pb.op("vector", lambda e: e.tensor_tensor(out=ctx.hc[1][:, 0:64], in0=tsc[:],
                                                      in1=ctx.hc[0][:, 0:64], op=ALU.mult),
                  reads=["l0tsc", "hc0"], writes=["hc1"], tag=us + "mk")
            pb.op("vector", lambda e: e.tensor_scalar(out=ctx.hc[0][:, 0:64], in0=ctx.hc[0][:, 0:64],
                                                      scalar1=1e30, scalar2=-1e30,
                                                      op0=ALU.mult, op1=ALU.add),
                  reads=["hc0"], writes=["hc0"], tag=us + "pn")
            pb.op("vector", lambda e: e.tensor_tensor(out=ctx.hc[1][:, 0:64], in0=ctx.hc[1][:, 0:64],
                                                      in1=ctx.hc[0][:, 0:64], op=ALU.add),
                  reads=["hc1", "hc0"], writes=["hc1"], tag=us + "ma")
            pb.op("vector", lambda e: e.reduce_max(out=c1[:, 0:1], in_=ctx.hc[1][:, 0:64],
                                                   axis=AX.X),
                  reads=["hc1"], writes=["col1"], tag=us + "tm")
            pb.op("vector", lambda e: e.tensor_scalar(out=ctx.hc[0][:, 0:64], in0=ctx.hc[1][:, 0:64],
                                                      scalar1=c1[:, 0:1], scalar2=None,
                                                      op0=ALU.is_equal),
                  reads=["hc1", "col1"], writes=["hc0"], tag=us + "tq")
            pb.op("vector", lambda e: e.tensor_tensor(out=ctx.hc[0][:, 0:64], in0=ctx.hc[0][:, 0:64],
                                                      in1=M[:, 64:128], op=ALU.mult),
                  reads=["hc0", "l0misc"], writes=["hc0"], tag=us + "tw")
            pb.op("vector", lambda e: e.tensor_scalar(out=ctx.hc[0][:, 0:64], in0=ctx.hc[0][:, 0:64],
                                                      scalar1=-1.0, scalar2=99.0,
                                                      op0=ALU.mult, op1=ALU.add),
                  reads=["hc0"], writes=["hc0"], tag=us + "tx")
            pb.op("vector", lambda e: e.tensor_reduce(out=c1[:, 1:2], in_=ctx.hc[0][:, 0:64],
                                                      axis=AX.X, op=ALU.min),
                  reads=["hc0"], writes=["col1"], tag=us + "ti")
            oh0 = tc * 256 + st * 64
            pb.op("vector", lambda e, oh0=oh0: e.tensor_scalar(
                      out=ctx.l0_ohf[:, oh0:oh0 + 64], in0=M[:, 0:64],
                      scalar1=c1[:, 1:2], scalar2=None, op0=ALU.is_equal),
                  reads=["l0misc", "col1"], writes=["l0ohf"], tag=us + "oh")
            pb.op("tensor", lambda e, oh0=oh0: e.transpose(ctx.ps[4][0:64, 0:P],
                                                           ctx.l0_ohf[:, oh0:oh0 + 64],
                                                           ctx.ident_f32[:]),
                  reads=["l0ohf", "identf"], writes=["pb4"], tag=us + "ot")
            ot0 = tc * TC + st * P
            pb.op("vector", lambda e, ot0=ot0: e.tensor_copy(
                      out=ctx.l0_ohtb[0:64, ot0:ot0 + P], in_=ctx.ps[4][0:64, 0:P]),
                  reads=["pb4"], writes=["l0ohtb"], tag=us + "ob")
            pb.op("scalar", lambda e, ot0=ot0: e.copy(
                      out=ctx.l0_ohtbf[0:64, ot0:ot0 + P], in_=ctx.ps[4][0:64, 0:P]),
                  reads=["pb4"], writes=["l0ohtbf"], tag=us + "of")
        for j in range(NDC):
            pb.op("tensor", lambda e, j=j, tc=tc: e.matmul(
                      ctx.ps[2][:], ctx.l0_exp[:, j * P:(j + 1) * P],
                      ctx.l0_ohtb[0:64, tc * TC:tc * TC + TC], start=True, stop=True),
                  reads=["l0exp", "l0ohtb"], writes=["pb2"], tag=f"{uq}ex{j}")
            pb.op("vector", lambda e, j=j: e.tensor_tensor(out=ctx.hc[0][:], in0=ctx.xt[j][:],
                                                           in1=ctx.ps[2][:], op=ALU.mult),
                  reads=[f"xt{j}", "pb2"], writes=["hc0"], tag=f"{uq}mm{j}")
            pb.op("tensor", lambda e, j=j: e.matmul(ctx.ps[3][0:32, :], ctx.l0_fold[:],
                                                    ctx.hc[0][:], start=(j == 0),
                                                    stop=(j == NDC - 1)),
                  reads=["l0fold", "hc0"], writes=["pb3"], tag=f"{uq}fd{j}")
        pb.op("vector", lambda e: e.tensor_copy(out=ctx.hc[1][0:32, :], in_=ctx.ps[3][0:32, :]),
              reads=["pb3"], writes=["hc1"], tag=uq + "xe")
        for st in range(4):
            us = f"{uq}x{st}_"
            q0 = st * P
            pb.op("tensor", lambda e, q0=q0: e.transpose(ctx.ps[4][:, 0:32],
                                                         ctx.hc[1][0:32, q0:q0 + P],
                                                         ctx.ident_f32[0:32, 0:32]),
                  reads=["hc1", "identf"], writes=["pb4"], tag=us + "tr")
            xs0 = tc * P + st * 32
            pb.op("vector", lambda e, xs0=xs0: e.tensor_copy(out=ctx.l0_xs[:, xs0:xs0 + 32],
                                                             in_=ctx.ps[4][:, 0:32]),
                  reads=["pb4"], writes=["l0xs"], tag=us + "xc")
            c0 = ctx.cols[0]
            pb.op("vector", lambda e, xs0=xs0: e.tensor_reduce(out=c0[:, 2:3],
                                                               in_=ctx.l0_xs[:, xs0:xs0 + 32],
                                                               axis=AX.X, op=ALU.min),
                  reads=["l0xs"], writes=["col0"], tag=us + "mn")
            pb.op("vector", lambda e, xs0=xs0: e.tensor_reduce(out=c0[:, 3:4],
                                                               in_=ctx.l0_xs[:, xs0:xs0 + 32],
                                                               axis=AX.X, op=ALU.max),
                  reads=["l0xs"], writes=["col0"], tag=us + "mx")
            pb.op("tensor", lambda e: e.transpose(ctx.ps[4][0:1, 0:P], c0[:, 2:3],
                                                  ctx.ident_f32[:]),
                  reads=["col0", "identf"], writes=["pb4"], tag=us + "t1")
            pb.op("vector", lambda e, q0=q0: e.tensor_copy(out=ctx.rows[6][0:1, q0:q0 + P],
                                                           in_=ctx.ps[4][0:1, 0:P]),
                  reads=["pb4"], writes=["row6"], tag=us + "r1")
            pb.op("tensor", lambda e: e.transpose(ctx.ps[4][0:1, 0:P], c0[:, 3:4],
                                                  ctx.ident_f32[:]),
                  reads=["col0", "identf"], writes=["pb4"], tag=us + "t2")
            pb.op("vector", lambda e, q0=q0: e.tensor_copy(out=ctx.rows[7][0:1, q0:q0 + P],
                                                           in_=ctx.ps[4][0:1, 0:P]),
                  reads=["pb4"], writes=["row7"], tag=us + "r2")
        for which, rw, colx, sgn in (("mn", "row6", 0, 1.0), ("mx", "row7", 1, -1.0)):
            pb.op("tensor", lambda e, rw=rw: e.matmul(ctx.ps[4][0:64, :],
                                                      ctx.ones_row_f32[0:1, 0:64],
                                                      ctx.rows[6 if rw == "row6" else 7][:],
                                                      start=True, stop=True),
                  reads=[rw, "ones_row_f32"], writes=["pb4"], tag=f"{uq}{which}b")
            pb.op("vector", lambda e, tc=tc: e.tensor_tensor(
                      out=ctx.hc[0][0:64, :], in0=ctx.ps[4][0:64, :],
                      in1=ctx.l0_ohtb[0:64, tc * TC:tc * TC + TC], op=ALU.mult),
                  reads=["pb4", "l0ohtb"], writes=["hc0"], tag=f"{uq}{which}m")
            pb.op("vector", lambda e, sgn=sgn, tc=tc: e.tensor_scalar(
                      out=ctx.hc[1][0:64, :], in0=ctx.l0_ohtb[0:64, tc * TC:tc * TC + TC],
                      scalar1=-sgn * 1e30, scalar2=sgn * 1e30,
                      op0=ALU.mult, op1=ALU.add),
                  reads=["l0ohtb"], writes=["hc1"], tag=f"{uq}{which}s")
            pb.op("vector", lambda e: e.tensor_tensor(out=ctx.hc[0][0:64, :],
                                                      in0=ctx.hc[0][0:64, :],
                                                      in1=ctx.hc[1][0:64, :], op=ALU.add),
                  reads=["hc0", "hc1"], writes=["hc0"], tag=f"{uq}{which}a")
            pb.op("vector", lambda e, sgn=sgn: e.tensor_reduce(
                      out=ctx.cols[2][0:64, 0:1], in_=ctx.hc[0][0:64, :],
                      axis=AX.X, op=(ALU.min if sgn > 0 else ALU.max)),
                  reads=["hc0"], writes=["col2"], tag=f"{uq}{which}r")
            pb.op("vector", lambda e, colx=colx, sgn=sgn: e.tensor_tensor(
                      out=ctx.l0_tm[0:64, colx:colx + 1], in0=ctx.l0_tm[0:64, colx:colx + 1],
                      in1=ctx.cols[2][0:64, 0:1], op=(ALU.min if sgn > 0 else ALU.max)),
                  reads=["l0tm", "col2"], writes=["l0tm"], tag=f"{uq}{which}u")

    pb.dma("gpsimd", out=ctx.l0_tmd, in_=ctx.l0_tm[:], reads=["l0tm"], writes=["l0tmd"],
           tag=u0 + "tmw")
    pb.collective("AllGather", ALU.bypass, [list(range(8))], [ctx.l0_tmd], [ctx.l0_tmg],
                  reads=["l0tmd"], writes=["l0tmg"], tag=u0 + "ag")
    gm = ctx.cols[3]
    pb.dma("gpsimd", out=ctx.hc[2][0:64, 0:32].rearrange("p (c m) -> p c m", m=4),
           in_=ctx.l0_tmg.rearrange("(c p) m -> p c m", c=8),
           reads=["l0tmg"], writes=["hc2"], tag=u0 + "gl")
    pb.op("vector", lambda e: e.tensor_reduce(
              out=gm[0:64, 0:1],
              in_=ctx.hc[2][0:64, 0:32].rearrange("p (c m) -> p c m", m=4)[:, :, 0:1],
              axis=AX.XY, op=ALU.min),
          reads=["hc2"], writes=["col3"], tag=u0 + "gmn")
    pb.op("vector", lambda e: e.tensor_reduce(
              out=gm[0:64, 1:2],
              in_=ctx.hc[2][0:64, 0:32].rearrange("p (c m) -> p c m", m=4)[:, :, 1:2],
              axis=AX.XY, op=ALU.max),
          reads=["hc2"], writes=["col3"], tag=u0 + "gmx")
    for cix, dst0 in ((0, 0), (1, 64)):
        pb.op("tensor", lambda e, cix=cix: e.transpose(ctx.ps[4][0:1, 0:64],
                                                       gm[0:64, cix:cix + 1],
                                                       ctx.ident_f32[0:64, 0:64]),
              reads=["col3", "identf"], writes=["pb4"], tag=f"{u0}gt{cix}")
        pb.op("vector", lambda e, cix=cix: e.tensor_copy(out=ctx.rows[6][0:1, 0:64],
                                                         in_=ctx.ps[4][0:1, 0:64]),
              reads=["pb4"], writes=["row6"], tag=f"{u0}gc{cix}")
        pb.op("tensor", lambda e: e.matmul(ctx.ps[4][:, 0:64], ctx.ones_row_f32[:],
                                           ctx.rows[6][0:1, 0:64], start=True, stop=True),
              reads=["row6", "ones_row_f32"], writes=["pb4"], tag=f"{u0}gb{cix}")
        pb.op("vector", lambda e, dst0=dst0: e.tensor_copy(out=ctx.l0_gmm[:, dst0:dst0 + 64],
                                                           in_=ctx.ps[4][:, 0:64]),
              reads=["pb4"], writes=["l0gmm"], tag=f"{u0}ge{cix}")

    for tc in range(NTC):
        t0 = tc * TC
        uq = f"{u0}b{tc}_"
        for st in range(4):
            us = f"{uq}s{st}_"
            oh0 = tc * 256 + st * 64
            xs0 = tc * P + st * 32
            c0, c1 = ctx.cols[0], ctx.cols[1]
            pb.op("vector", lambda e, oh0=oh0: e.tensor_tensor(
                      out=ctx.hc[0][:, 0:64], in0=ctx.l0_ohf[:, oh0:oh0 + 64],
                      in1=ctx.l0_gmm[:, 0:64], op=ALU.mult),
                  reads=["l0ohf", "l0gmm"], writes=["hc0"], tag=us + "m1")
            pb.op("vector", lambda e: e.tensor_reduce(out=c0[:, 0:1], in_=ctx.hc[0][:, 0:64],
                                                      axis=AX.X, op=ALU.add),
                  reads=["hc0"], writes=["col0"], tag=us + "m2")
            pb.op("vector", lambda e, oh0=oh0: e.tensor_tensor(
                      out=ctx.hc[0][:, 0:64], in0=ctx.l0_ohf[:, oh0:oh0 + 64],
                      in1=ctx.l0_gmm[:, 64:128], op=ALU.mult),
                  reads=["l0ohf", "l0gmm"], writes=["hc0"], tag=us + "m3")
            pb.op("vector", lambda e: e.tensor_reduce(out=c0[:, 1:2], in_=ctx.hc[0][:, 0:64],
                                                      axis=AX.X, op=ALU.add),
                  reads=["hc0"], writes=["col0"], tag=us + "m4")
            pb.op("vector", lambda e: e.tensor_tensor(out=c0[:, 2:3], in0=c0[:, 1:2],
                                                      in1=c0[:, 0:1], op=ALU.subtract),
                  reads=["col0"], writes=["col0"], tag=us + "dn")
            pb.op("vector", lambda e: e.tensor_scalar(out=c0[:, 2:3], in0=c0[:, 2:3],
                                                      scalar1=1e-8, scalar2=None, op0=ALU.add),
                  reads=["col0"], writes=["col0"], tag=us + "de")
            pb.op("vector", lambda e: e.reciprocal(out=c0[:, 3:4], in_=c0[:, 2:3]),
                  reads=["col0"], writes=["col0"], tag=us + "dr")
            pb.op("vector", lambda e, xs0=xs0: e.tensor_scalar(
                      out=ctx.hc[0][:, 0:32], in0=ctx.l0_xs[:, xs0:xs0 + 32],
                      scalar1=c0[:, 0:1], scalar2=None, op0=ALU.subtract),
                  reads=["l0xs", "col0"], writes=["hc0"], tag=us + "u1")
            pb.op("vector", lambda e: e.tensor_scalar(out=ctx.hc[0][:, 0:32], in0=ctx.hc[0][:, 0:32],
                                                      scalar1=c0[:, 3:4], scalar2=None,
                                                      op0=ALU.mult),
                  reads=["hc0", "col0"], writes=["hc0"], tag=us + "u2")
            pb.op("vector", lambda e: e.tensor_scalar(out=ctx.hc[0][:, 0:32], in0=ctx.hc[0][:, 0:32],
                                                      scalar1=1.0 - 1e-6, scalar2=0.0,
                                                      op0=ALU.min, op1=ALU.max),
                  reads=["hc0"], writes=["hc0"], tag=us + "u3")
            u16c = tc * P + st * 32
            pb.op("vector", lambda e, u16c=u16c: e.tensor_scalar(
                      out=ctx.l0_u16[:, u16c:u16c + 32], in0=ctx.hc[0][:, 0:32],
                      scalar1=16.0, scalar2=None, op0=ALU.mult),
                  reads=["hc0"], writes=["l0u16"], tag=us + "u4")
            pb.op("vector", lambda e: e.memset(ctx.hc[3][:, 0:32], 0.0),
                  writes=["hc3"], tag=us + "g0")
            for g in range(1, 16):
                pb.op("vector", lambda e, g=g, u16c=u16c: e.tensor_scalar(
                          out=ctx.hc[0][:, 0:32], in0=ctx.l0_u16[:, u16c:u16c + 32],
                          scalar1=float(g), scalar2=None, op0=ALU.is_ge),
                      reads=["l0u16"], writes=["hc0"], tag=us + f"g{g}")
                pb.op("gpsimd", lambda e: e.tensor_tensor(out=ctx.hc[3][:, 0:32],
                                                          in0=ctx.hc[3][:, 0:32],
                                                          in1=ctx.hc[0][:, 0:32], op=ALU.add),
                      reads=["hc3", "hc0"], writes=["hc3"], tag=us + f"ga{g}")
            ot0 = tc * TC + st * P
            pb.op("tensor", lambda e, ot0=ot0: e.matmul(
                      ctx.ps[5][:], ctx.l0_ohtbf[0:64, ot0:ot0 + P], ctx.l0_tbl[0:64, 0:512],
                      start=True, stop=True),
                  reads=["l0ohtbf", "l0tbl"], writes=["pb5"], tag=us + "tb")
            pb.op("tensor", lambda e, ot0=ot0: e.matmul(
                      ctx.ps[6][:], ctx.l0_ohtbf[0:64, ot0:ot0 + P], ctx.l0_tbl[0:64, 512:1024],
                      start=True, stop=True),
                  reads=["l0ohtbf", "l0tbl"], writes=["pb6"], tag=us + "tsl")
            pb.op("tensor", lambda e, ot0=ot0: e.matmul(
                      ctx.ps[7][:, 0:16], ctx.l0_ohtbf[0:64, ot0:ot0 + P],
                      ctx.l0_tbl[0:64, 1024:1040], start=True, stop=True),
                  reads=["l0ohtbf", "l0tbl"], writes=["pb7"], tag=us + "to")
            gi_exp = ctx.hc[3][:, 0:32].unsqueeze(2).broadcast_to((P, 32, 16))
            pb.op("vector", lambda e: e.tensor_tensor(
                      out=ctx.hc[1][:, 0:512],
                      in0=ctx.l0_g16[:, 0:512].rearrange("p (j g) -> p j g", g=16),
                      in1=gi_exp, op=ALU.is_equal),
                  reads=["l0g16", "hc3"], writes=["hc1"], tag=us + "se")
            for which, bank, dst0 in (("b", 5, 0), ("s", 6, 32)):
                pb.op("vector", lambda e, bank=bank: e.tensor_tensor(
                          out=ctx.hc[0][:, 0:512], in0=ctx.hc[1][:, 0:512],
                          in1=ctx.ps[bank][:], op=ALU.mult),
                      reads=["hc1", f"pb{bank}"], writes=["hc0"], tag=us + which + "m")
                cur = 512
                src_t, src_k = ctx.hc[0], "hc0"
                dst_t, dst_k = ctx.hc[2], "hc2"
                gsz = 16
                while gsz > 1:
                    half = gsz // 2
                    pb.op("vector", lambda e, src_t=src_t, dst_t=dst_t, gsz=gsz, half=half: e.tensor_tensor(
                              out=dst_t[:, 0:32 * half].rearrange("p (j g) -> p j g", g=half),
                              in0=src_t[:, 0:32 * gsz].rearrange("p (j g) -> p j g", g=gsz)[:, :, 0:half],
                              in1=src_t[:, 0:32 * gsz].rearrange("p (j g) -> p j g", g=gsz)[:, :, half:gsz],
                              op=ALU.add),
                          reads=[src_k], writes=[dst_k], tag=us + which + f"h{gsz}")
                    src_t, dst_t = dst_t, src_t
                    src_k, dst_k = dst_k, src_k
                    gsz = half
                pb.op("vector", lambda e, src_t=src_t, dst0=dst0: e.tensor_copy(
                          out=ctx.l0_bs[:, dst0:dst0 + 32], in_=src_t[:, 0:32]),
                      reads=[src_k], writes=["l0bs"], tag=us + which + "c")
            pb.op("vector", lambda e, u16c=u16c: e.tensor_tensor(
                      out=ctx.hc[0][:, 0:32], in0=ctx.l0_u16[:, u16c:u16c + 32],
                      in1=ctx.hc[3][:, 0:32], op=ALU.subtract),
                  reads=["l0u16", "hc3"], writes=["hc0"], tag=us + "xl")
            pb.op("vector", lambda e: e.tensor_tensor(out=ctx.hc[0][:, 0:32],
                                                      in0=ctx.l0_bs[:, 32:64],
                                                      in1=ctx.hc[0][:, 0:32], op=ALU.mult),
                  reads=["l0bs", "hc0"], writes=["hc0"], tag=us + "sx")
            pb.op("vector", lambda e: e.tensor_tensor(out=ctx.hc[0][:, 0:32],
                                                      in0=ctx.l0_bs[:, 0:32],
                                                      in1=ctx.hc[0][:, 0:32], op=ALU.add),
                  reads=["l0bs", "hc0"], writes=["hc0"], tag=us + "bx")
            pb.op("vector", lambda e: e.tensor_copy(out=c0[:, 4:5], in_=ctx.ps[7][:, 0:1]),
                  reads=["pb7"], writes=["col0"], tag=us + "oc")
            pb.op("vector", lambda e: e.tensor_scalar(out=ctx.hc[0][:, 0:32], in0=ctx.hc[0][:, 0:32],
                                                      scalar1=c0[:, 4:5], scalar2=None,
                                                      op0=ALU.mult),
                  reads=["hc0", "col0"], writes=["hc0"], tag=us + "dl")
            pb.op("tensor", lambda e: e.transpose(ctx.ps[4][0:32, 0:P], ctx.hc[0][:, 0:32],
                                                  ctx.ident_f32[:]),
                  reads=["hc0", "identf"], writes=["pb4"], tag=us + "dt")
            pb.op("vector", lambda e, st=st: e.tensor_copy(
                      out=ctx.l0_dT[0:32, st * P:(st + 1) * P], in_=ctx.ps[4][0:32, 0:P]),
                  reads=["pb4"], writes=["l0dT"], tag=us + "dc")
        pb.op("tensor", lambda e: e.matmul(ctx.ps[3][:], ctx.l0_rep[:], ctx.l0_dT[0:32, :],
                                           start=True, stop=True),
              reads=["l0rep", "l0dT"], writes=["pb3"], tag=uq + "rp")
        pb.op("vector", lambda e: e.tensor_copy(out=ctx.hc[1][:], in_=ctx.ps[3][:]),
              reads=["pb3"], writes=["hc1"], tag=uq + "rc")
        for j in range(NDC):
            pb.op("tensor", lambda e, j=j, tc=tc: e.matmul(
                      ctx.ps[2][:], ctx.l0_exp[:, j * P:(j + 1) * P],
                      ctx.l0_ohtb[0:64, tc * TC:tc * TC + TC], start=True, stop=True),
                  reads=["l0exp", "l0ohtb"], writes=["pb2"], tag=f"{uq}e2{j}")
            pb.op("vector", lambda e, j=j: e.tensor_tensor(out=ctx.hc[0][:], in0=ctx.hc[1][:],
                                                           in1=ctx.ps[2][:], op=ALU.mult),
                  reads=["hc1", "pb2"], writes=["hc0"], tag=f"{uq}sc{j}")
            pb.dma("gpsimd", out=ctx.xt[j][:], in_=x_in[j * P:(j + 1) * P, t0:t0 + TC],
                   writes=[f"xt{j}"], tag=f"{uq}lr{j}")
            pb.op("vector", lambda e, j=j: e.scalar_tensor_tensor(
                      out=ctx.xt[j][:], in0=ctx.hc[0][:], scalar=0.5, in1=ctx.xt[j][:],
                      op0=ALU.mult, op1=ALU.add),
                  reads=["hc0", f"xt{j}"], writes=[f"xt{j}"], tag=f"{uq}r1{j}")
            pb.op("vector", lambda e, j=j, tc=tc: e.scalar_tensor_tensor(
                      out=ctx.xt[j][:], in0=ctx.hb[tc * 16 + j][:], scalar=0.5, in1=ctx.xt[j][:],
                      op0=ALU.mult, op1=ALU.add),
                  reads=[f"hb{tc * 16 + j}", f"xt{j}"], writes=[f"xt{j}"], tag=f"{uq}r2{j}")
            pb.dma("gpsimd", out=x_out[j * P:(j + 1) * P, t0:t0 + TC], in_=ctx.xt[j][:],
                   reads=[f"xt{j}"], writes=["xout"], tag=f"{uq}so{j}")


def _host_prep(inputs):
    h = {}
    f32 = np.float32
    x = np.asarray(inputs["L0_attn_X"], f32)
    shards = []
    for c in range(NCORE):
        b, hh = c // 2, c % 2
        xt = np.ascontiguousarray(x[b, hh * TCORE:(hh + 1) * TCORE, :].T)
        shards.append(xt)
    h["x_shards"] = shards

    common = {}
    common["c_ones_col"] = np.ones((P, 1), bf)
    common["c_ones_row"] = np.ones((1, P), bf)
    common["c_ones_tc"] = np.ones((1, TC), bf)
    common["c_ones_col_f32"] = np.ones((P, 1), np.float32)
    common["c_ones_row_f32"] = np.ones((1, P), np.float32)
    eps = np.array([1e-5, 1e-6, 1e-8, 0.0, 0.0, 0.0, 0.0, 0.0], np.float32)
    common["c_eps_row"] = eps.reshape(1, 8)
    common["c_eps_col"] = np.broadcast_to(eps, (P, 8)).copy()
    common["c_ident_f32"] = np.eye(P, dtype=f32)
    common["c_ident_bf"] = np.eye(P, dtype=bf)

    common["L1w1T"] = np.ascontiguousarray(np.asarray(inputs["L1_ffn_w1"], f32).T)
    common["L1w2T"] = np.ascontiguousarray(np.asarray(inputs["L1_ffn_w2"], f32).T)
    common["L1b1T"] = np.ascontiguousarray(np.asarray(inputs["L1_ffn_b1"], f32).reshape(64, P).T)
    common["L1b2row"] = np.asarray(inputs["L1_ffn_b2"], f32).reshape(1, D)

    cm = np.zeros((4 * P, TC), f32)
    for v in range(4):
        kl = np.arange(P)[:, None]
        ql = np.arange(TC)[None, :]
        cm[v * P:(v + 1) * P] = np.where(v * P + kl <= ql, 0.0, NEG30).astype(f32)
    common["c_cmask"] = cm

    percore = [dict() for _ in range(NCORE)]
    for c in range(NCORE):
        hh = c % 2
        ms = np.zeros((P, 2), f32)
        ms[:, hh] = 1.0
        percore[c]["c_msel"] = ms

    wqkvT = np.ascontiguousarray(np.asarray(inputs["L1_attn_w_qkv"], f32).T)
    for c in range(NCORE):
        hh = c % 2
        percore[c]["L1wq"] = np.ascontiguousarray(wqkvT[:, 1024 * hh:1024 * (hh + 1)])
        percore[c]["L1wk"] = np.ascontiguousarray(wqkvT[:, 2048 + 1024 * hh:2048 + 1024 * (hh + 1)])
        percore[c]["L1wv"] = np.ascontiguousarray(wqkvT[:, 4096 + 1024 * hh:4096 + 1024 * (hh + 1)])
    common["L1wo"] = np.ascontiguousarray(np.asarray(inputs["L1_attn_w_o"], f32).T)

    for c in range(NCORE):
        hh = c % 2
        for nm, w in (("L3wq", "L3_attn_Wq"), ("L3wk", "L3_attn_Wk"), ("L3wv", "L3_attn_Wv")):
            wT = np.asarray(inputs[w]).T
            percore[c][nm] = np.ascontiguousarray(wT[:, 1024 * hh:1024 * (hh + 1)])
        for nm, b in (("l3a_bq", "L3_attn_bq"), ("l3a_bk", "L3_attn_bk")):
            bb = np.asarray(inputs[b], f32)[1024 * hh:1024 * (hh + 1)]
            percore[c][nm] = np.ascontiguousarray(bb.reshape(8, P).T)
        bv = np.asarray(inputs["L3_attn_bv"], f32)[1024 * hh:1024 * (hh + 1)]
        percore[c]["l3a_bv"] = bv.reshape(1, 1024).astype(bf)
    common["L3wo"] = np.ascontiguousarray(np.asarray(inputs["L3_attn_Wo"]).T)
    common["l3a_bo"] = np.ascontiguousarray(np.asarray(inputs["L3_attn_bo"], f32).reshape(16, P).T)

    for c in range(NCORE):
        hh = c % 2
        percore[c]["L0wq"] = np.ascontiguousarray(np.asarray(inputs["L0_attn_Wq"], f32)[:, 2048 * hh:2048 * (hh + 1)])
        percore[c]["L0wk"] = np.ascontiguousarray(np.asarray(inputs["L0_attn_Wk"], f32)[:, 1024 * hh:1024 * (hh + 1)])
        percore[c]["L0wv"] = np.ascontiguousarray(np.asarray(inputs["L0_attn_Wv"], f32)[:, 1024 * hh:1024 * (hh + 1)])
    common["L0wo"] = np.ascontiguousarray(np.asarray(inputs["L0_attn_Wo"], f32))
    inv_freq = 1.0 / (10000.0 ** (np.arange(0, 256, 2, dtype=np.int32).astype(f32) / 256.0))
    pos = np.arange(SEQ, dtype=f32)
    fr = pos[None, :] * inv_freq[:, None]
    common["l0a_cos"] = np.cos(fr).astype(f32)
    common["l0a_sin"] = np.sin(fr).astype(f32)

    ii, jj = np.meshgrid(np.arange(P), np.arange(P), indexing="ij")
    h128 = np.where((np.bitwise_count(np.bitwise_and(ii, jj)) % 2) == 0, 1.0, -1.0)
    common["c_h128_bf"] = h128.astype(bf)
    common["c_h128_f16"] = h128.astype(np.float16)
    common["L2wg"] = np.ascontiguousarray(np.asarray(inputs["L2_ffn_W_gate"], f32).T)
    common["L2wu"] = np.ascontiguousarray(np.asarray(inputs["L2_ffn_W_up"], f32).T)
    common["L2wd"] = np.ascontiguousarray(np.asarray(inputs["L2_ffn_W_down"], f32).T)
    common["L2suu"] = np.ascontiguousarray(np.asarray(inputs["L2_ffn_SU_ug"], f32).reshape(16, P).T)
    common["L2sud"] = np.ascontiguousarray(np.asarray(inputs["L2_ffn_SU_dp"], f32).reshape(64, P).T)
    wsug = np.asarray(inputs["L2_ffn_Wscale_ug"], f32)
    common["L2wsu"] = np.ascontiguousarray(wsug[:8192].reshape(64, P).T)
    common["L2wsg"] = np.ascontiguousarray(wsug[8192:].reshape(64, P).T)
    common["L2wsd"] = np.ascontiguousarray(np.asarray(inputs["L2_ffn_Wscale_dp"], f32).reshape(16, P).T)

    for c in range(NCORE):
        hh = c % 2
        for nm, w in (("L2wq", "L2_attn_Wq"), ("L2wk", "L2_attn_Wk"), ("L2wv", "L2_attn_Wv")):
            wT = np.asarray(inputs[w], f32).T
            percore[c][nm] = np.ascontiguousarray(wT[:, 1024 * hh:1024 * (hh + 1)])
    common["L2wo"] = np.ascontiguousarray(np.asarray(inputs["L2_attn_Wo"], f32).T)
    inv_freq2 = 1.0 / (10000.0 ** (np.arange(0, 128, 2, dtype=np.int32).astype(f32) / 128.0))
    fr2 = pos[None, :] * inv_freq2[:, None]
    common["l2a_cos"] = np.concatenate([np.cos(fr2), np.cos(fr2)], axis=0).astype(f32)
    common["l2a_sin"] = np.concatenate([-np.sin(fr2), np.sin(fr2)], axis=0).astype(f32)
    sw = np.zeros((P, P), f32)
    for m in range(P):
        sw[(m + 64) % P, m] = 1.0
    common["c_swap64"] = sw.astype(bf)
    common["l2a_onw"] = np.asarray(inputs["L2_attn_o_norm_w"], f32).reshape(P, 1)
    common["c_iota16"] = np.broadcast_to(np.arange(16, dtype=f32), (P, 16)).copy()
    common["c_neg_row"] = np.full((1, P), NEG30, bf)
    gm = np.zeros((P, 16 * 32), f32)
    for qt in range(32):
        ct = qt // 2
        row = np.zeros(16, f32)
        row[ct + 1:] = NEG30
        row[ct] = 1e30
        gm[:, qt * 16:(qt + 1) * 16] = row[None, :]
    common["c_l2gm"] = gm

    common["c_iota8"] = np.broadcast_to(np.arange(8, dtype=f32), (P, 8)).copy()
    common["L3rmsw"] = np.ascontiguousarray(np.asarray(inputs["L3_ffn_rms_weight"], f32).reshape(16, P).T)
    common["L3router"] = np.ascontiguousarray(np.asarray(inputs["L3_ffn_router_w"], f32))
    common["L3wg"] = np.ascontiguousarray(np.asarray(inputs["L3_ffn_w_gate"], f32).reshape(8 * D, 1024))
    common["L3wu"] = np.ascontiguousarray(np.asarray(inputs["L3_ffn_w_up"], f32).reshape(8 * D, 1024))
    common["L3wd"] = np.ascontiguousarray(np.asarray(inputs["L3_ffn_w_down"], f32).reshape(8 * 1024, D))
    common["L3swg"] = np.ascontiguousarray(np.asarray(inputs["L3_ffn_sw_gate"], f32))
    common["L3swu"] = np.ascontiguousarray(np.asarray(inputs["L3_ffn_sw_up"], f32))
    common["L3swd"] = np.ascontiguousarray(np.asarray(inputs["L3_ffn_sw_down"], f32))

    common["L0gam"] = np.ascontiguousarray(np.asarray(inputs["L0_ffn_ln_gamma"], f32).reshape(16, P).T)
    common["L0bet"] = np.ascontiguousarray(np.asarray(inputs["L0_ffn_ln_beta"], f32).reshape(16, P).T)
    tbl = np.zeros((64, 1040), f32)
    tbl[:, 0:512] = np.asarray(inputs["L0_ffn_spline_bases"], f32).reshape(64, 512)
    tbl[:, 512:1024] = np.asarray(inputs["L0_ffn_spline_slopes"], f32).reshape(64, 512)
    tbl[:, 1024] = np.asarray(inputs["L0_ffn_output_scale"], f32)
    common["L0tbl"] = tbl
    misc = np.zeros((P, 320), f32)
    misc[:, 0:64] = np.arange(64)[None, :]
    misc[:, 64:128] = 99.0 - np.arange(64)[None, :]
    misc[:, 128:192] = (np.arange(64) // 8)[None, :]
    for c4 in range(4):
        misc[:, 192 + c4] = (np.arange(P) // 32 == c4).astype(f32)
    misc[:, 256:264] = 99.0 - np.arange(8)[None, :]
    misc[:, 264:272] = np.arange(8)[None, :]
    common["c_l0misc"] = misc
    common["c_l0g16"] = np.broadcast_to(np.tile(np.arange(16, dtype=f32), 32), (P, 512)).copy()
    ex = np.zeros((64, D), f32)
    for t64 in range(64):
        ex[t64, t64 * 32:(t64 + 1) * 32] = 1.0
    common["c_l0exp"] = ex.astype(bf)
    common["c_l0fold"] = (np.arange(P)[:, None] % 32 == np.arange(32)[None, :]).astype(f32)
    common["c_l0rep"] = (np.arange(32)[:, None] == np.arange(P)[None, :] % 32).astype(f32)
    common["c_l0asn"] = (np.arange(64)[:, None] // 8 == np.arange(8)[None, :]).astype(bf)

    h["common"] = common
    h["percore"] = percore
    return h


_PROGS = {}

SUBLAYER_CHAIN = ["l0ffn", "l0attn", "l0ffn", "l1ffn", "l1attn", "l1ffn",
                  "l2ffn", "l2attn", "l2ffn", "l3ffn", "l3attn", "l3ffn"]

COMMON_NAMES = ["c_ones_col", "c_ones_row", "c_ones_tc", "c_ones_col_f32",
                "c_ones_row_f32", "c_ident_f32", "c_ident_bf", "c_eps_row", "c_eps_col"]

SUBLAYER_INPUTS = {
    "l1ffn": ["L1w1T", "L1w2T", "L1b1T", "L1b2row"],
    "l1attn": ["c_cmask", "c_msel", "L1wq", "L1wk", "L1wv", "L1wo"],
    "l3attn": ["c_msel", "L3wq", "L3wk", "L3wv", "L3wo",
               "l3a_bq", "l3a_bk", "l3a_bv", "l3a_bo"],
    "l0attn": ["c_cmask", "c_msel", "L0wq", "L0wk", "L0wv", "L0wo",
               "l0a_cos", "l0a_sin"],
    "l3ffn": ["c_iota8", "c_ident_f32", "L3rmsw", "L3router", "L3wg", "L3wu",
              "L3wd", "L3swg", "L3swu", "L3swd"],
    "l2ffn": ["c_h128_bf", "c_h128_f16", "L2wg", "L2wu", "L2wd", "L2suu",
              "L2sud", "L2wsu", "L2wsg", "L2wsd"],
    "l2attn": ["c_cmask", "c_msel", "c_iota16", "c_neg_row", "c_l2gm",
               "c_ident_f32", "c_swap64", "L2wq", "L2wk", "L2wv", "L2wo",
               "l2a_cos", "l2a_sin", "l2a_onw"],
    "l0ffn": ["c_ident_f32", "L0gam", "L0bet", "L0tbl", "c_l0misc", "c_l0g16",
              "c_l0exp", "c_l0fold", "c_l0rep", "c_l0asn"],
}

PERCORE_NAMES = {"c_msel", "L1wq", "L1wk", "L1wv", "L3wq", "L3wk", "L3wv",
                 "l3a_bq", "l3a_bk", "l3a_bv", "L0wq", "L0wk", "L0wv",
                 "L2wq", "L2wk", "L2wv"}


def _build_sublayer_prog(kind):
    nc = bass.Bass("TRN2", target_bir_lowering=False)
    stack = ExitStack()
    names = list(dict.fromkeys(COMMON_NAMES + SUBLAYER_INPUTS[kind]))
    ins = {}
    shapes = _input_shapes()
    for n in names:
        shp, dt = shapes[n]
        ins[n] = nc.dram_tensor(n, list(shp), dt, kind="ExternalInput").ap()
    ins["xin"] = nc.dram_tensor("xin", [D, TCORE], F32, kind="ExternalInput").ap()
    xout = nc.dram_tensor("xout", [D, TCORE], F32, kind="ExternalOutput").ap()
    pb = ProgBuilder(nc, stack)
    ctx = Ctx(nc, stack, pb, ins)
    if kind == "l1ffn":
        setup_l1ffn_consts(ctx)
        emit_l1ffn(ctx, ins["xin"], xout)
    elif kind in ATTN_CFGS:
        cfg = ATTN_CFGS[kind]
        for wn in ("wq", "wk", "wv", "wo"):
            ctx.stage_bf16(cfg[wn])
        setup_attn_consts(ctx, cfg)
        emit_attn_sublayer(ctx, ins["xin"], xout, cfg)
    elif kind == "l3ffn":
        for wn in ("L3wg", "L3wu", "L3wd", "L3swg", "L3swu", "L3swd", "L3router"):
            ctx.stage_bf16(wn)
        setup_l3ffn_consts(ctx)
        emit_l3ffn(ctx, ins["xin"], xout)
    elif kind == "l0ffn":
        emit_l0ffn(ctx, ins["xin"], xout)
    elif kind == "l2ffn":
        ctx.stage_bf16("L2wg")
        ctx.stage_bf16("L2wu")
        ctx.stage_bf16("L2wd", dt=F16)
        setup_l2ffn_consts(ctx)
        emit_l2ffn(ctx, ins["xin"], xout)
    else:
        raise NotImplementedError(kind)
    pb.finalize()
    return nc, stack, names


_SHAPES_CACHE = None


def _input_shapes():
    global _SHAPES_CACHE
    if _SHAPES_CACHE is None:
        _SHAPES_CACHE = {
            "c_ones_col": ((P, 1), BF16),
            "c_ones_row": ((1, P), BF16),
            "c_ones_tc": ((1, TC), BF16),
            "c_ones_col_f32": ((P, 1), F32),
            "c_ones_row_f32": ((1, P), F32),
            "c_eps_row": ((1, 8), F32),
            "c_eps_col": ((P, 8), F32),
            "c_ident_f32": ((P, P), F32),
            "c_ident_bf": ((P, P), BF16),
            "L1w1T": ((D, 8192), F32),
            "L1w2T": ((8192, D), F32),
            "L1b1T": ((P, 64), F32),
            "L1b2row": ((1, D), F32),
            "c_cmask": ((4 * P, TC), F32),
            "c_msel": ((P, 2), F32),
            "L1wq": ((D, 1024), F32),
            "L1wk": ((D, 1024), F32),
            "L1wv": ((D, 1024), F32),
            "L1wo": ((D, D), F32),
            "L3wq": ((D, 1024), BF16),
            "L3wk": ((D, 1024), BF16),
            "L3wv": ((D, 1024), BF16),
            "L3wo": ((D, D), BF16),
            "l3a_bq": ((P, 8), F32),
            "l3a_bk": ((P, 8), F32),
            "l3a_bv": ((1, 1024), BF16),
            "l3a_bo": ((P, 16), F32),
            "L0wq": ((D, 2048), F32),
            "L0wk": ((D, 1024), F32),
            "L0wv": ((D, 1024), F32),
            "L0wo": ((4096, D), F32),
            "l0a_cos": ((128, SEQ), F32),
            "l0a_sin": ((128, SEQ), F32),
            "c_iota8": ((P, 8), F32),
            "L3rmsw": ((P, 16), F32),
            "L3router": ((D, 8), F32),
            "L3wg": ((8 * D, 1024), F32),
            "L3wu": ((8 * D, 1024), F32),
            "L3wd": ((8 * 1024, D), F32),
            "L3swg": ((D, 1024), F32),
            "L3swu": ((D, 1024), F32),
            "L3swd": ((1024, D), F32),
            "c_h128_bf": ((P, P), BF16),
            "c_h128_f16": ((P, P), F16),
            "L2wg": ((D, 8192), F32),
            "L2wu": ((D, 8192), F32),
            "L2wd": ((8192, D), F32),
            "L2suu": ((P, 16), F32),
            "L2sud": ((P, 64), F32),
            "L2wsu": ((P, 64), F32),
            "L2wsg": ((P, 64), F32),
            "L2wsd": ((P, 16), F32),
            "c_iota16": ((P, 16), F32),
            "c_neg_row": ((1, P), BF16),
            "c_l2gm": ((P, 16 * 32), F32),
            "c_swap64": ((P, P), BF16),
            "L2wq": ((D, 1024), F32),
            "L2wk": ((D, 1024), F32),
            "L2wv": ((D, 1024), F32),
            "L2wo": ((D, D), F32),
            "l2a_cos": ((P, SEQ), F32),
            "l2a_sin": ((P, SEQ), F32),
            "l2a_onw": ((P, 1), F32),
            "L0gam": ((P, 16), F32),
            "L0bet": ((P, 16), F32),
            "L0tbl": ((64, 1040), F32),
            "c_l0misc": ((P, 320), F32),
            "c_l0g16": ((P, 512), F32),
            "c_l0exp": ((64, D), BF16),
            "c_l0fold": ((P, 32), F32),
            "c_l0rep": ((32, P), F32),
            "c_l0asn": ((64, 8), BF16),
        }
    return _SHAPES_CACHE


def run_sublayer(kind, x_shards, host):
    if kind not in _PROGS:
        _PROGS[kind] = _build_sublayer_prog(kind)
    nc, stack, names = _PROGS[kind]
    in_maps = []
    for c in range(NCORE):
        m = {}
        for n in names:
            m[n] = (host["percore"][c][n] if n in PERCORE_NAMES
                    else host["common"][n])
        m["xin"] = x_shards[c]
        in_maps.append(m)
    res = run_bass_kernel_spmd(nc, in_maps, core_ids=list(range(NCORE)))
    return [np.asarray(res.results[c]["xout"]) for c in range(NCORE)]


def _reassemble(x_shards):
    out = np.empty((BATCH, SEQ, D), np.float32)
    for c in range(NCORE):
        b, hh = c // 2, c % 2
        out[b, hh * TCORE:(hh + 1) * TCORE, :] = x_shards[c].T
    return out


def kernel(**inputs):
    host = _host_prep(inputs)
    xs = host["x_shards"]
    for kind in SUBLAYER_CHAIN:
        xs = run_sublayer(kind, xs, host)
    return _reassemble(xs)
